# version 03
# baseline (speedup 1.0000x reference)
import os
os.environ.setdefault("BASS_DISABLE_FRAME_TO_TRACEBACK", "1")

import contextlib
import numpy as np

import concourse.bass as bass
import concourse.mybir as mybir
from concourse.bass_utils import run_bass_kernel_spmd

F32 = mybir.dt.float32
BF16 = mybir.dt.bfloat16
I32 = mybir.dt.int32
AF = mybir.ActivationFunctionType
ALU = mybir.AluOpType
AX = mybir.AxisListType

NCORES = 8
S = 2048
D = 1024
NT = S // 128
HEADS = 16
HID = 4096
RH = 1024
NE = 8
EPS = 1e-5
NEG = -1.0e30
TW = 2048

ENGINES = ("tensor", "vector", "scalar", "gpsimd", "sync")
DMA_QUEUES = {"sync": "qS", "gpsimd": "qG", "scalar": "qA"}


class Op:
    __slots__ = ("eng", "fn", "drain", "waits_dma", "waits_op", "inc_needed",
                 "sem_val", "seq", "qsem", "qval")

    def __init__(self, eng, fn, seq):
        self.eng = eng
        self.fn = fn
        self.drain = False
        self.waits_dma = {}
        self.waits_op = []
        self.inc_needed = False
        self.sem_val = None
        self.seq = seq
        self.qsem = None
        self.qval = None


class Buf:
    __slots__ = ("w", "r")

    def __init__(self):
        self.w = None
        self.r = []


class Emitter:
    def __init__(self):
        self.ops = {e: [] for e in ENGINES}
        self.dma_cnt = {q: 0 for q in DMA_QUEUES.values()}
        self.drain_cover = {e: -1 for e in ENGINES}

    def _deps(self, op, reads, writes):
        toks = []
        for b in reads:
            if b.w is not None:
                toks.append(b.w)
        for b in writes:
            if b.w is not None:
                toks.append(b.w)
            toks.extend(b.r)
        for t in toks:
            if t[0] == "dma":
                _, q, v = t
                op.waits_dma[q] = max(op.waits_dma.get(q, 0), v)
            else:
                p = t[1]
                if p.eng == op.eng:
                    if op.eng != "tensor" and p.seq > self.drain_cover[op.eng]:
                        op.drain = True
                else:
                    p.inc_needed = True
                    op.waits_op.append(p)
        if op.drain:
            self.drain_cover[op.eng] = op.seq - 1

    def emit(self, eng, fn, reads=(), writes=()):
        op = Op(eng, fn, len(self.ops[eng]))
        self._deps(op, reads, writes)
        self.ops[eng].append(op)
        tok = ("op", op)
        for b in writes:
            b.w = tok
            b.r = []
        for b in reads:
            b.r.append(tok)
        return op

    def dma(self, eng, out_ap, in_ap, reads=(), writes=()):
        q = DMA_QUEUES[eng]
        op = Op(eng, None, len(self.ops[eng]))
        self._deps(op, reads, writes)
        self.dma_cnt[q] += 16
        op.qsem = q
        op.qval = self.dma_cnt[q]

        def fn(h, out_ap=out_ap, in_ap=in_ap):
            return h.dma_start(out=out_ap, in_=in_ap)
        op.fn = fn
        self.ops[eng].append(op)
        tok = ("dma", q, op.qval)
        for b in writes:
            b.w = tok
            b.r = []
        for b in reads:
            b.r.append(tok)
        return op

    def finalize_and_replay(self, block, sems):
        for e in ENGINES:
            v = 0
            for op in self.ops[e]:
                if op.inc_needed:
                    v += 1
                    op.sem_val = v
        esem = {e: sems["s_" + e] for e in ENGINES}
        qsems = {q: sems[q] for q in self.dma_cnt}

        def body(e):
            ops = self.ops[e]

            def run(h):
                watermark = {}
                for op in ops:
                    targets = {}
                    for q, v in op.waits_dma.items():
                        if watermark.get(q, 0) < v:
                            targets[qsems[q]] = max(targets.get(qsems[q], 0), v)
                            watermark[q] = v
                    for p in op.waits_op:
                        key = "s_" + p.eng
                        if watermark.get(key, 0) < p.sem_val:
                            targets[esem[p.eng]] = max(
                                targets.get(esem[p.eng], 0), p.sem_val)
                            watermark[key] = p.sem_val
                    for s, v in targets.items():
                        h.wait_ge(s, v)
                    if op.drain:
                        h.drain()
                    instr = op.fn(h)
                    if op.qsem is not None:
                        instr.then_inc(qsems[op.qsem], 16)
                    elif op.inc_needed:
                        instr.then_inc(esem[op.eng], 1)
            return run

        for e in ENGINES:
            getattr(block, e)(body(e))


def ceil_div(a, b):
    return (a + b - 1) // b


def build_program():
    nc = bass.Bass("TRN2", target_bir_lowering=False, debug=False)

    dx = nc.dram_tensor("x_in", [S, D], F32, kind="ExternalInput").ap()
    dout = nc.dram_tensor("x_out", [S, D], F32, kind="ExternalOutput").ap()
    dwqT = nc.dram_tensor("wqT", [D, D], F32, kind="ExternalInput").ap()
    dwkT = nc.dram_tensor("wkT", [D, D], F32, kind="ExternalInput").ap()
    dwvT = nc.dram_tensor("wvT", [D, D], F32, kind="ExternalInput").ap()
    dwoT = nc.dram_tensor("woT", [D, D], F32, kind="ExternalInput").ap()
    dw1T = nc.dram_tensor("w1T", [D, HID], F32, kind="ExternalInput").ap()
    dw2T = nc.dram_tensor("w2T", [D, HID], F32, kind="ExternalInput").ap()
    dw3sT = nc.dram_tensor("w3sT", [HID, D], F32, kind="ExternalInput").ap()
    dw12T = nc.dram_tensor("w12T", [NE, D, 2 * RH], F32, kind="ExternalInput").ap()
    dw3eT = nc.dram_tensor("w3eT", [NE, RH, D], F32, kind="ExternalInput").ap()
    drt = nc.dram_tensor("routerT", [D, NE], F32, kind="ExternalInput").ap()
    debias = nc.dram_tensor("ebias", [1, NE], F32, kind="ExternalInput").ap()

    A = nc.alloc_sbuf_tensor
    x_sb = A("x_sb", [128, NT * 1024], F32).ap()
    actT_h = A("actT_h", [128, 8 * TW], BF16).ap()
    actT_l = A("actT_l", [128, 8 * TW], BF16).ap()
    hb_h = A("hb_h", [128, 2 * 1024], BF16).ap()
    hb_l = A("hb_l", [128, 2 * 1024], BF16).ap()
    w3h = A("w3h", [128, 2 * 1024], BF16).ap()
    w3l = A("w3l", [128, 2 * 1024], BF16).ap()
    wsh = A("wsh", [128, 4 * 1024], BF16).ap()
    wsl = A("wsl", [128, 4 * 1024], BF16).ap()
    wf = A("wf", [128, 2048], F32).ap()
    h1s = A("h1s", [128, 1024], F32).ap()
    gst = A("gst", [128, 1024], F32).ap()
    srow = A("srow", [128, S], F32).ap()
    ptb = A("ptb", [128, 2 * 512], F32).ap()
    ot_h = A("ot_h", [128, NT * 128], BF16).ap()
    ot_l = A("ot_l", [128, NT * 128], BF16).ap()
    rs512 = A("rs512", [128, 512], F32).ap()
    mask512 = A("mask512", [128, 128], F32).ap()
    v_sb = wf
    ident = A("ident", [128, 128], F32).ap()
    iotaD = A("iotaD", [128, 512], I32).ap()
    ones1 = A("ones1", [1, 128], F32).ap()
    ocomb = A("ocomb", [128, 128], F32).ap()
    rt_sb = A("rt_sb", [128, 8 * NE], F32).ap()
    biasB = A("biasB", [128, NE], F32).ap()
    bias1 = A("bias1", [1, NE], F32).ap()
    gates05 = A("gates05", [128, NT * NE], F32).ap()
    stats6 = A("stats6", [128, 12], F32).ap()
    stats2 = A("stats2", [128, 2], F32).ap()
    sqv = A("sqv", [128, 1], F32).ap()
    rstd = A("rstd", [128, 1], F32).ap()
    sc_sb = A("sc_sb", [128, NE], F32).ap()
    selb = A("selb", [128, NE], F32).ap()
    tmask = A("tmask", [128, NE], F32).ap()
    masked = A("masked", [128, NE], F32).ap()
    maskf = A("maskf", [128, NE], F32).ap()
    m1c = A("m1c", [128, 1], F32).ap()
    m2c = A("m2c", [128, 1], F32).ap()
    cntc = A("cntc", [128, 1], F32).ap()
    selc = A("selc", [128, 1], F32).ap()
    selc2 = A("selc2", [128, 1], F32).ap()
    mrow = A("mrow", [128, 1], F32).ap()
    negm = A("negm", [128, 1], F32).ap()
    sumr = A("sumr", [128, 1], F32).ap()
    recip = A("recip", [128, 2], F32).ap()

    qth, qtl = hb_h, hb_l
    kth, ktl = w3h, w3l

    psum = nc.alloc_psum_tensor("psum", [128, 4096], F32).ap()

    def bank(i, w=512):
        return psum[:, 512 * i:512 * i + w]

    em = Emitter()

    XB = [Buf() for _ in range(NT)]
    AB = [Buf() for _ in range(NT)]
    ABL = [Buf() for _ in range(NT)]
    PB = [Buf() for _ in range(8)]
    HB = [Buf() for _ in range(2)]
    HBL = [Buf() for _ in range(2)]
    W3C = [Buf() for _ in range(2)]
    W3L = [Buf() for _ in range(2)]
    WS4 = [Buf() for _ in range(4)]
    WS4L = [Buf() for _ in range(4)]
    WFB = [Buf(), Buf()]
    GB = [Buf() for _ in range(NT)]
    PT2 = [Buf() for _ in range(2)]
    STATS = Buf(); SC = Buf(); TOPB = Buf()
    SRB = Buf()
    OTB = Buf(); OTBL = Buf(); OCB = Buf(); RSB = Buf(); H1S = Buf()
    GSTB = Buf()
    CONST = Buf(); RTB = Buf(); BB = Buf(); RECB = Buf()
    QTBL = [HB[0], HB[1], HBL[0], HBL[1]]
    KTBL = [W3C[0], W3C[1], W3L[0], W3L[1]]
    VBL = WFB

    em.emit("gpsimd", lambda h: h.iota(iotaD, pattern=[[1, 512]], base=0,
                                       channel_multiplier=-1), writes=[CONST])
    em.emit("vector", lambda h: h.tensor_scalar(
        ident, iotaD[:, 0:128], 0, None, ALU.is_equal, ALU.bypass),
        reads=[CONST], writes=[CONST])
    rt_view = drt.rearrange("(dc p) e -> p dc e", p=128)
    em.dma("sync", rt_sb.rearrange("p (dc e) -> p dc e", e=NE), rt_view,
           writes=[RTB])
    em.dma("sync", bias1, debias, writes=[BB])
    em.emit("vector", lambda h: h.memset(ones1, 1.0), writes=[CONST])
    em.emit("tensor", lambda h: h.matmul(bank(0, NE), ones1, bias1,
                                         start=True, stop=True),
            reads=[BB, CONST], writes=[PB[0]])
    em.emit("vector", lambda h: h.tensor_copy(biasB, bank(0, NE)),
            reads=[PB[0]], writes=[BB, PB[0]])
    em.emit("vector", lambda h: h.memset(ocomb, 0.0), writes=[OCB])
    em.emit("vector", lambda h: h.tensor_scalar(
        mask512, iotaD[:, 0:128], 0, NEG, ALU.is_gt, ALU.mult),
        reads=[CONST], writes=[CONST])
    xv = dx.rearrange("(t p) d -> p t d", p=128)
    for ti in range(NT):
        em.dma("gpsimd", x_sb[:, 1024 * ti:1024 * (ti + 1)], xv[:, ti, :],
               writes=[XB[ti]])

    athv = actT_h.rearrange("p (dc t) -> p dc t", t=TW)
    atlv = actT_l.rearrange("p (dc t) -> p dc t", t=TW)

    def ln_tile(ti):
        tcol = 128 * ti
        xt = x_sb[:, 1024 * ti:1024 * (ti + 1)]
        em.emit("vector", lambda h: h.bn_stats(stats6[:, 0:6], xt[:, 0:512]),
                reads=[XB[ti]], writes=[STATS])
        em.emit("vector", lambda h: h.bn_stats(stats6[:, 6:12], xt[:, 512:1024]),
                reads=[XB[ti]], writes=[STATS])
        em.emit("vector", lambda h: h.bn_aggr(stats2, stats6),
                reads=[STATS], writes=[STATS])
        em.emit("vector", lambda h: h.tensor_scalar(
            sqv, stats2[:, 1:2], EPS, None, ALU.add, ALU.bypass),
            reads=[STATS], writes=[STATS])
        em.emit("scalar", lambda h: h.activation(out=sqv, in_=sqv,
                                                 func=AF.Sqrt),
                reads=[STATS], writes=[STATS])
        em.emit("vector", lambda h: h.reciprocal(rstd, sqv),
                reads=[STATS], writes=[STATS])
        em.emit("vector", lambda h: h.tensor_scalar(
            h1s, xt, stats2[:, 0:1], rstd, ALU.subtract, ALU.mult),
            reads=[XB[ti], STATS], writes=[H1S])
        for half in range(2):
            b = 6 + half
            for j in range(4):
                dc = 4 * half + j
                em.emit("tensor", lambda h, dc=dc, j=j, b=b: h.transpose(
                    bank(b)[:, 128 * j:128 * (j + 1)],
                    h1s[:, 128 * dc:128 * (dc + 1)], ident),
                    reads=[H1S, CONST], writes=[PB[b]])
            em.emit("vector", lambda h, half=half, b=b: h.tensor_copy(
                gst[:, 512 * half:512 * (half + 1)], bank(b)),
                reads=[PB[b]], writes=[GSTB, PB[b]])
        gv = gst.rearrange("p (dc t) -> p dc t", t=128)
        em.emit("gpsimd", lambda h, tcol=tcol: h.tensor_copy(
            athv[:, 0:8, tcol:tcol + 128], gv),
            reads=[GSTB], writes=[AB[ti]])
        em.emit("gpsimd", lambda h, tcol=tcol: h.tensor_tensor(
            atlv[:, 0:8, tcol:tcol + 128], gv,
            athv[:, 0:8, tcol:tcol + 128], ALU.subtract),
            reads=[GSTB, AB[ti]], writes=[ABL[ti]])

    def router_tile(ti):
        for dc in range(8):
            em.emit("tensor", lambda h, dc=dc: h.matmul(
                bank(5, NE), gst[:, 128 * dc:128 * (dc + 1)],
                rt_sb[:, dc * NE:(dc + 1) * NE],
                start=(dc == 0), stop=(dc == 7)),
                reads=[GSTB, RTB], writes=[PB[5]])
        em.emit("scalar", lambda h: h.activation(out=sc_sb, in_=bank(5, NE),
                                                 func=AF.Sigmoid),
                reads=[PB[5]], writes=[SC, PB[5]])
        em.emit("vector", lambda h: h.tensor_tensor(selb, sc_sb, biasB, ALU.add),
                reads=[SC, BB], writes=[TOPB])
        em.emit("vector", lambda h: h.reduce_max(m1c, selb, axis=AX.X),
                reads=[TOPB], writes=[TOPB])
        em.emit("vector", lambda h: h.tensor_scalar(
            tmask, selb, m1c, None, ALU.is_ge, ALU.bypass),
            reads=[TOPB], writes=[TOPB])
        em.emit("vector", lambda h: h.scalar_tensor_tensor(
            masked, tmask, NEG, selb, ALU.mult, ALU.add),
            reads=[TOPB], writes=[TOPB])
        em.emit("vector", lambda h: h.reduce_max(m2c, masked, axis=AX.X),
                reads=[TOPB], writes=[TOPB])
        em.emit("vector", lambda h: h.tensor_scalar(
            maskf, selb, m2c, None, ALU.is_ge, ALU.bypass),
            reads=[TOPB], writes=[TOPB])
        em.emit("vector", lambda h: h.tensor_reduce(
            cntc, tmask, axis=AX.X, op=ALU.add),
            reads=[TOPB], writes=[TOPB])
        em.emit("vector", lambda h: h.tensor_scalar(
            selc, cntc, 1.5, None, ALU.is_ge, ALU.bypass),
            reads=[TOPB], writes=[TOPB])
        em.emit("vector", lambda h: h.tensor_scalar(
            selc2, selc, -1.0, 1.0, ALU.mult, ALU.add),
            reads=[TOPB], writes=[TOPB])
        em.emit("vector", lambda h: h.tensor_scalar(
            maskf, maskf, selc2, None, ALU.mult, ALU.bypass),
            reads=[TOPB], writes=[TOPB])
        em.emit("vector", lambda h: h.scalar_tensor_tensor(
            maskf, tmask, selc, maskf, ALU.mult, ALU.add),
            reads=[TOPB], writes=[TOPB])
        g = gates05[:, NE * ti:NE * (ti + 1)]
        em.emit("vector", lambda h, g=g: h.scalar_tensor_tensor(
            g, sc_sb, 0.5, maskf, ALU.mult, ALU.mult),
            reads=[SC, TOPB], writes=[GB[ti]])

    wpar = [0]

    def wload_split(dram_ap, wh_ap, wl_ap, wbuf, wbufl):
        par = wpar[0]
        wpar[0] ^= 1
        wfs = wf[:, 1024 * par:1024 * (par + 1)]
        wfd = wfs
        if len(dram_ap.shape) == 3:
            wfd = wfs.rearrange("p (dc c) -> p dc c", c=dram_ap.shape[2])
        em.dma("sync", wfd, dram_ap, writes=[WFB[par]])
        em.emit("gpsimd", lambda h: h.tensor_copy(wh_ap, wfs),
                reads=[WFB[par]], writes=[wbuf])
        em.emit("gpsimd", lambda h: h.tensor_tensor(wl_ap, wfs, wh_ap,
                                                    ALU.subtract),
                reads=[WFB[par], wbuf], writes=[wbufl, WFB[par]])

    def h_stage(j, whb, wlb, c1h, c1l, c2h, c2l, tis):
        toff = 128 * tis[0]
        for ts2 in range(2):
            rdh = [AB[t] for t in tis[4 * ts2:4 * ts2 + 4]]
            rdl = [ABL[t] for t in tis[4 * ts2:4 * ts2 + 4]]
            for wi, (cw, bb) in enumerate(((c1h, 4 + ts2), (c2h, 6 + ts2))):
                cl = c1l if cw is c1h else c2l
                for dc in range(8):
                    ah = actT_h[:, dc * TW + toff + 512 * ts2:
                                dc * TW + toff + 512 * (ts2 + 1)]
                    al = actT_l[:, dc * TW + toff + 512 * ts2:
                                dc * TW + toff + 512 * (ts2 + 1)]
                    for pi, (lw, rw, rb) in enumerate((
                            (cw, ah, [whb[wi]] + rdh),
                            (cw, al, [whb[wi]] + rdl),
                            (cl, ah, [wlb[wi]] + rdh))):
                        em.emit("tensor",
                                lambda h, dc=dc, bb=bb, lw=lw, rw=rw, pi=pi:
                                h.matmul(bank(bb),
                                         lw[:, 128 * dc:128 * (dc + 1)], rw,
                                         start=(dc == 0 and pi == 0),
                                         stop=(dc == 7 and pi == 2)),
                                reads=rb, writes=[PB[bb]])
        em.emit("scalar", lambda h: h.activation(
            out=h1s, in_=psum[:, 2048:3072], func=AF.Silu),
            reads=[PB[4], PB[5]], writes=[H1S, PB[4], PB[5]])
        em.emit("vector", lambda h: h.tensor_tensor(
            gst, psum[:, 3072:4096], h1s, ALU.mult),
            reads=[PB[6], PB[7], H1S], writes=[GSTB, PB[6], PB[7]])
        em.emit("gpsimd", lambda h, j=j: h.tensor_copy(
            hb_h[:, 1024 * j:1024 * (j + 1)], gst),
            reads=[GSTB], writes=[HB[j]])
        em.emit("gpsimd", lambda h, j=j: h.tensor_tensor(
            hb_l[:, 1024 * j:1024 * (j + 1)], gst,
            hb_h[:, 1024 * j:1024 * (j + 1)], ALU.subtract),
            reads=[GSTB, HB[j]], writes=[HBL[j]])

    def out_stage(tis, gcol_fn):
        for k, ti in enumerate(tis):
            for dh in range(2):
                b = (2 * k + dh) % 4
                nmm = 0
                for j in range(2):
                    lh = hb_h[:, 1024 * j + 128 * k:1024 * j + 128 * (k + 1)]
                    ll = hb_l[:, 1024 * j + 128 * k:1024 * j + 128 * (k + 1)]
                    rh_ = w3h[:, 1024 * j + 512 * dh:1024 * j + 512 * (dh + 1)]
                    rl_ = w3l[:, 1024 * j + 512 * dh:1024 * j + 512 * (dh + 1)]
                    for lw, rw, rb in ((lh, rh_, [HB[j], W3C[j]]),
                                       (lh, rl_, [HB[j], W3L[j]]),
                                       (ll, rh_, [HBL[j], W3C[j]])):
                        nmm += 1
                        em.emit("tensor",
                                lambda h, lw=lw, rw=rw, b=b, s=(nmm == 1),
                                e=(nmm == 6):
                                h.matmul(bank(b), lw, rw, start=s, stop=e),
                                reads=rb, writes=[PB[b]])
                gcol, grd = gcol_fn(ti)
                xs = x_sb[:, 1024 * ti + 512 * dh:1024 * ti + 512 * (dh + 1)]
                em.emit("vector", lambda h, b=b, gcol=gcol, xs=xs:
                        h.scalar_tensor_tensor(xs, bank(b), gcol, xs,
                                               ALU.mult, ALU.add),
                        reads=[PB[b], XB[ti]] + grd, writes=[XB[ti], PB[b]])

    def ffn_call():
        for ti in range(NT):
            ln_tile(ti)
            router_tile(ti)
        w1v = dw1T.rearrange("(dc p) hh -> p dc hh", p=128)
        w2v = dw2T.rearrange("(dc p) hh -> p dc hh", p=128)
        for half in range(2):
            tis = list(range(8 * half, 8 * half + 8))
            for g in range(16):
                for j in range(2):
                    hc = 2 * g + j
                    c1h = wsh[:, 1024 * j:1024 * (j + 1)]
                    c1l = wsl[:, 1024 * j:1024 * (j + 1)]
                    c2h = wsh[:, 2048 + 1024 * j:3072 + 1024 * j]
                    c2l = wsl[:, 2048 + 1024 * j:3072 + 1024 * j]
                    wload_split(
                        w1v[:, :, 128 * hc:128 * (hc + 1)], c1h, c1l,
                        WS4[j], WS4L[j])
                    wload_split(
                        w2v[:, :, 128 * hc:128 * (hc + 1)], c2h, c2l,
                        WS4[2 + j], WS4L[2 + j])
                    h_stage(j, (WS4[j], WS4[2 + j]), (WS4L[j], WS4L[2 + j]),
                            c1h, c1l, c2h, c2l, tis)
                    wload_split(dw3sT[128 * hc:128 * (hc + 1), :],
                                w3h[:, 1024 * j:1024 * (j + 1)],
                                w3l[:, 1024 * j:1024 * (j + 1)],
                                W3C[j], W3L[j])
                out_stage(tis, lambda ti: (0.5, []))
            for e in range(NE):
                wv = dw12T[e].rearrange("(dc p) hh -> p dc hh", p=128)
                for g2 in range(4):
                    for j in range(2):
                        rhc = 2 * g2 + j
                        c1h = wsh[:, 2048 * j:2048 * j + 1024]
                        c1l = wsl[:, 2048 * j:2048 * j + 1024]
                        c2h = wsh[:, 2048 * j + 1024:2048 * (j + 1)]
                        c2l = wsl[:, 2048 * j + 1024:2048 * (j + 1)]
                        wload_split(
                            wv[:, :, 128 * rhc:128 * (rhc + 1)],
                            c1h, c1l, WS4[2 * j], WS4L[2 * j])
                        wload_split(
                            wv[:, :, RH + 128 * rhc:RH + 128 * (rhc + 1)],
                            c2h, c2l, WS4[2 * j + 1], WS4L[2 * j + 1])
                        h_stage(j, (WS4[2 * j], WS4[2 * j + 1]),
                                (WS4L[2 * j], WS4L[2 * j + 1]),
                                c1h, c1l, c2h, c2l, tis)
                        wload_split(dw3eT[e][128 * rhc:128 * (rhc + 1), :],
                                    w3h[:, 1024 * j:1024 * (j + 1)],
                                    w3l[:, 1024 * j:1024 * (j + 1)],
                                    W3C[j], W3L[j])
                    out_stage(tis, lambda ti, e=e: (
                        gates05[:, NE * ti + e:NE * ti + e + 1], [GB[ti]]))

    def attn_block():
        for ti in range(NT):
            ln_tile(ti)
        slopes = [2.0 ** (-(hh + 1) / 2.0) for hh in range(HEADS)]
        for p in range(8):
            wslices = []
            for qi_, dw in enumerate((dwqT, dwkT, dwvT)):
                wvw = dw.rearrange("(dc p2) o -> p2 dc o", p2=128)
                ch = wsh[:, 1024 * qi_:1024 * (qi_ + 1)]
                cl = wsl[:, 1024 * qi_:1024 * (qi_ + 1)]
                wload_split(
                    wvw[:, :, 128 * p:128 * (p + 1)], ch, cl,
                    WS4[qi_], WS4L[qi_])
                wslices.append((ch, cl))
            woh = wsh[:, 3072:4096]
            wol = wsl[:, 3072:4096]
            wload_split(dwoT[128 * p:128 * (p + 1), :], woh, wol,
                        WS4[3], WS4L[3])
            (wqh, wql), (wkh, wkl), (wvh, wvl) = wslices
            for ts4 in range(4):
                rdh = [AB[t] for t in range(4 * ts4, 4 * ts4 + 4)]
                rdl = [ABL[t] for t in range(4 * ts4, 4 * ts4 + 4)]
                for wi, (dsth, dstl, hbl, lbl, boff) in enumerate(
                        ((qth, qtl, [HB[0], HB[1]], [HBL[0], HBL[1]], 4),
                         (kth, ktl, [W3C[0], W3C[1]], [W3L[0], W3L[1]], 6))):
                    wh_ = wsh[:, 1024 * wi:1024 * (wi + 1)]
                    wl_ = wsl[:, 1024 * wi:1024 * (wi + 1)]
                    b = boff + ts4 % 2
                    for dc in range(8):
                        ah = actT_h[:, dc * TW + 512 * ts4:
                                    dc * TW + 512 * (ts4 + 1)]
                        al = actT_l[:, dc * TW + 512 * ts4:
                                    dc * TW + 512 * (ts4 + 1)]
                        for pi, (lw, rw, rb) in enumerate(
                                ((wh_, ah, [WS4[wi]] + rdh),
                                 (wh_, al, [WS4[wi]] + rdl),
                                 (wl_, ah, [WS4L[wi]] + rdh))):
                            em.emit("tensor",
                                    lambda h, dc=dc, b=b, lw=lw, rw=rw, pi=pi:
                                    h.matmul(bank(b),
                                             lw[:, 128 * dc:128 * (dc + 1)],
                                             rw,
                                             start=(dc == 0 and pi == 0),
                                             stop=(dc == 7 and pi == 2)),
                                    reads=rb, writes=[PB[b]])
                    em.emit("scalar", lambda h, dsth=dsth, ts4=ts4, b=b:
                            h.copy(dsth[:, 512 * ts4:512 * (ts4 + 1)],
                                   bank(b)),
                            reads=[PB[b]], writes=list(hbl))
                    em.emit("vector",
                            lambda h, dsth=dsth, dstl=dstl, ts4=ts4, b=b:
                            h.tensor_tensor(dstl[:, 512 * ts4:512 * (ts4 + 1)],
                                            bank(b),
                                            dsth[:, 512 * ts4:512 * (ts4 + 1)],
                                            ALU.subtract),
                            reads=[PB[b]] + list(hbl),
                            writes=list(lbl) + [PB[b]])
            for k in range(NT):
                b = k % 2
                for dc in range(8):
                    ah = athv[:, dc, 128 * k:128 * (k + 1)]
                    al = atlv[:, dc, 128 * k:128 * (k + 1)]
                    vh = wvh[:, 128 * dc:128 * (dc + 1)]
                    vl = wvl[:, 128 * dc:128 * (dc + 1)]
                    for pi, (lw, rw, rb) in enumerate(
                            ((ah, vh, [WS4[2], AB[k]]),
                             (ah, vl, [WS4L[2], AB[k]]),
                             (al, vh, [WS4[2], ABL[k]]))):
                        em.emit("tensor",
                                lambda h, b=b, lw=lw, rw=rw, dc=dc, pi=pi:
                                h.matmul(bank(b, 128), lw, rw,
                                         start=(dc == 0 and pi == 0),
                                         stop=(dc == 7 and pi == 2)),
                                reads=rb, writes=[PB[b]])
                em.emit("scalar", lambda h, k=k, b=b: h.copy(
                    v_sb[:, 128 * k:128 * (k + 1)], bank(b, 128)),
                    reads=[PB[b]], writes=[VBL[k // 8], PB[b]])
            for hi in range(2):
                hof = 64 * hi
                slope = slopes[2 * p + hi]
                em.emit("vector", lambda h, slope=slope: h.tensor_scalar(
                    rs512, iotaD, slope, None, ALU.mult, ALU.bypass),
                    reads=[CONST], writes=[RSB])
                for qi in range(NT):
                    kmax = 128 * (qi + 1)
                    nkj = ceil_div(kmax, 512)
                    for kj in range(nkj):
                        w = min(512, kmax - 512 * kj)
                        b = 4 + (kj % 2)
                        qh_ = qth[hof:hof + 64, 128 * qi:128 * (qi + 1)]
                        ql_ = qtl[hof:hof + 64, 128 * qi:128 * (qi + 1)]
                        kh_ = kth[hof:hof + 64, 512 * kj:512 * kj + w]
                        kl_ = ktl[hof:hof + 64, 512 * kj:512 * kj + w]
                        for pi, (lw, rw) in enumerate(
                                ((qh_, kh_), (qh_, kl_), (ql_, kh_))):
                            em.emit("tensor",
                                    lambda h, w=w, b=b, lw=lw, rw=rw, pi=pi:
                                    h.matmul(bank(b, w), lw, rw,
                                             start=(pi == 0), stop=(pi == 2)),
                                    reads=QTBL + KTBL, writes=[PB[b]])
                        em.emit("vector", lambda h, kj=kj, w=w, b=b, s=slope:
                                h.tensor_scalar(
                                    srow[:, 512 * kj:512 * kj + w],
                                    bank(b, w), 0.125, s * 512.0 * kj,
                                    ALU.mult, ALU.add),
                                reads=[PB[b]], writes=[SRB, PB[b]])
                        em.emit("vector", lambda h, kj=kj, w=w:
                                h.tensor_tensor(
                                    srow[:, 512 * kj:512 * kj + w],
                                    srow[:, 512 * kj:512 * kj + w],
                                    rs512[:, 0:w], ALU.add),
                                reads=[SRB, RSB], writes=[SRB])
                    em.emit("vector", lambda h, kmax=kmax: h.tensor_tensor(
                        srow[:, kmax - 128:kmax], srow[:, kmax - 128:kmax],
                        mask512, ALU.add),
                        reads=[SRB, CONST], writes=[SRB])
                    em.emit("vector", lambda h, kmax=kmax: h.reduce_max(
                        mrow, srow[:, 0:kmax], axis=AX.X),
                        reads=[SRB], writes=[STATS])
                    em.emit("vector", lambda h: h.tensor_scalar(
                        negm, mrow, -1.0, None, ALU.mult, ALU.bypass),
                        reads=[STATS], writes=[STATS])
                    em.emit("scalar", lambda h, kmax=kmax: h.activation(
                        out=srow[:, 0:kmax], in_=srow[:, 0:kmax], func=AF.Exp,
                        bias=negm, scale=1.0, accum_out=sumr),
                        reads=[SRB, STATS], writes=[SRB, STATS])
                    em.emit("vector", lambda h, hi=hi: h.reciprocal(
                        recip[:, hi:hi + 1], sumr),
                        reads=[STATS], writes=[RECB])
                    nkt = kmax // 128
                    for kt4 in range(ceil_div(nkt, 4)):
                        b = 6 + (kt4 % 2)
                        n = min(4, nkt - 4 * kt4)
                        sl = PT2[kt4 % 2]
                        for j in range(n):
                            kt = 4 * kt4 + j
                            em.emit("tensor", lambda h, kt=kt, j=j, b=b:
                                    h.transpose(
                                        bank(b)[:, 128 * j:128 * (j + 1)],
                                        srow[:, 128 * kt:128 * (kt + 1)],
                                        ident),
                                    reads=[SRB, CONST], writes=[PB[b]])
                        em.emit("vector", lambda h, kt4=kt4, b=b, n=n:
                                h.tensor_copy(
                                    ptb[:, 512 * (kt4 % 2):
                                        512 * (kt4 % 2) + 128 * n],
                                    bank(b, 128 * n)),
                                reads=[PB[b]], writes=[sl, PB[b]])
                        for j in range(n):
                            kt = 4 * kt4 + j
                            em.emit("tensor",
                                    lambda h, kt=kt, j=j, kt4=kt4, hof=hof,
                                    hi=hi, nkt=nkt:
                                    h.matmul(
                                        bank(3, 128)[:, 64 * hi:64 * (hi + 1)],
                                        ptb[:, 512 * (kt4 % 2) + 128 * j:
                                            512 * (kt4 % 2) + 128 * (j + 1)],
                                        v_sb[:, 128 * kt + hof:
                                             128 * kt + hof + 64],
                                        start=(kt == 0), stop=(kt == nkt - 1)),
                                    reads=[sl, VBL[kt // 8]], writes=[PB[3]])
                    em.emit("vector", lambda h, hi=hi: h.tensor_scalar(
                        ocomb[:, 64 * hi:64 * (hi + 1)],
                        bank(3, 128)[:, 64 * hi:64 * (hi + 1)],
                        recip[:, hi:hi + 1], None, ALU.mult, ALU.bypass),
                        reads=[PB[3], RECB], writes=[OCB, PB[3]])
                    em.emit("tensor", lambda h: h.transpose(bank(2, 128),
                                                            ocomb, ident),
                            reads=[OCB, CONST], writes=[PB[2]])
                    em.emit("scalar", lambda h, qi=qi, hof=hof: h.copy(
                        ot_h[hof:hof + 64, 128 * qi:128 * (qi + 1)],
                        bank(2, 128)[hof:hof + 64, :]),
                        reads=[PB[2]], writes=[OTB])
                    em.emit("vector", lambda h, qi=qi, hof=hof:
                            h.tensor_tensor(
                                ot_l[hof:hof + 64, 128 * qi:128 * (qi + 1)],
                                bank(2, 128)[hof:hof + 64, :],
                                ot_h[hof:hof + 64, 128 * qi:128 * (qi + 1)],
                                ALU.subtract),
                            reads=[PB[2], OTB], writes=[OTBL, PB[2]])
            for k in range(NT):
                for dh in range(2):
                    b = (2 * k + dh) % 2
                    oh_ = ot_h[:, 128 * k:128 * (k + 1)]
                    ol_ = ot_l[:, 128 * k:128 * (k + 1)]
                    rh_ = woh[:, 512 * dh:512 * (dh + 1)]
                    rl_ = wol[:, 512 * dh:512 * (dh + 1)]
                    for pi, (lw, rw, rb) in enumerate(
                            ((oh_, rh_, [OTB, WS4[3]]),
                             (oh_, rl_, [OTB, WS4L[3]]),
                             (ol_, rh_, [OTBL, WS4[3]]))):
                        em.emit("tensor",
                                lambda h, b=b, lw=lw, rw=rw, pi=pi:
                                h.matmul(bank(b), lw, rw,
                                         start=(pi == 0), stop=(pi == 2)),
                                reads=rb, writes=[PB[b]])
                    xs = x_sb[:, 1024 * k + 512 * dh:1024 * k + 512 * (dh + 1)]
                    em.emit("vector", lambda h, b=b, xs=xs:
                            h.scalar_tensor_tensor(xs, bank(b), 1.0, xs,
                                                   ALU.mult, ALU.add),
                            reads=[PB[b], XB[k]], writes=[XB[k], PB[b]])

    ffn_call()
    attn_block()
    ffn_call()
    ov = dout.rearrange("(t p) d -> p t d", p=128)
    for ti in range(NT):
        em.dma("gpsimd", ov[:, ti, :], x_sb[:, 1024 * ti:1024 * (ti + 1)],
               reads=[XB[ti]])
    final_q = em.dma_cnt["qG"]

    stack = contextlib.ExitStack()
    sems = {}
    for name in ["s_tensor", "s_vector", "s_scalar", "s_gpsimd", "s_sync",
                 "qS", "qG", "qA"]:
        sems[name] = stack.enter_context(nc.semaphore(name))
    block = stack.enter_context(nc.Block())
    fin = Op("gpsimd", lambda h: h.nop(), len(em.ops["gpsimd"]))
    fin.waits_dma["qG"] = final_q
    em.ops["gpsimd"].append(fin)
    em.finalize_and_replay(block, sems)
    stack.close()

    nc.m.detect_race_conditions = False
    return nc


_CACHE = {}


def _get_nc():
    if "full" not in _CACHE:
        _CACHE["full"] = build_program()
    return _CACHE["full"]


def kernel(L0_attn_x, L0_attn_W_Q, L0_attn_W_K, L0_attn_W_V, L0_attn_W_O,
           L0_ffn_w12, L0_ffn_w3, L0_ffn_shared_w1, L0_ffn_shared_w2,
           L0_ffn_shared_w3, L0_ffn_router_w, L0_ffn_expert_bias):
    x = np.ascontiguousarray(np.asarray(L0_attn_x, dtype=np.float32))

    def T(a):
        return np.ascontiguousarray(np.asarray(a, dtype=np.float32).T)

    w = {
        "wqT": T(L0_attn_W_Q), "wkT": T(L0_attn_W_K), "wvT": T(L0_attn_W_V),
        "woT": T(L0_attn_W_O),
        "w1T": T(L0_ffn_shared_w1), "w2T": T(L0_ffn_shared_w2),
        "w3sT": T(L0_ffn_shared_w3),
        "w12T": np.ascontiguousarray(
            np.asarray(L0_ffn_w12, dtype=np.float32).transpose(0, 2, 1)),
        "w3eT": np.ascontiguousarray(
            np.asarray(L0_ffn_w3, dtype=np.float32).transpose(0, 2, 1)),
        "routerT": T(L0_ffn_router_w),
        "ebias": np.ascontiguousarray(
            np.asarray(L0_ffn_expert_bias, dtype=np.float32).reshape(1, NE)),
    }
    nc = _get_nc()
    in_maps = [dict(w, x_in=np.ascontiguousarray(x[b])) for b in range(NCORES)]
    res = run_bass_kernel_spmd(nc, in_maps, core_ids=list(range(NCORES)))
    out = np.stack([res.results[b]["x_out"] for b in range(NCORES)], axis=0)
    return out.astype(np.float32)


# version 04
# speedup vs baseline: 1.1678x; 1.1678x over previous
import os
os.environ.setdefault("BASS_DISABLE_FRAME_TO_TRACEBACK", "1")

import contextlib
import numpy as np

import concourse.bass as bass
import concourse.mybir as mybir
from concourse.bass_utils import run_bass_kernel_spmd

F32 = mybir.dt.float32
BF16 = mybir.dt.bfloat16
I32 = mybir.dt.int32
AF = mybir.ActivationFunctionType
ALU = mybir.AluOpType
AX = mybir.AxisListType

NCORES = 8
S = 2048
D = 1024
NT = S // 128
HEADS = 16
HID = 4096
RH = 1024
NE = 8
EPS = 1e-5
NEG = -1.0e30
TW = 2048

ENGINES = ("tensor", "vector", "scalar", "gpsimd", "sync")
DMA_QUEUES = {"sync": "qS", "gpsimd": "qG", "scalar": "qA"}


class Op:
    __slots__ = ("eng", "fn", "drain", "waits_dma", "waits_op", "inc_needed",
                 "sem_val", "seq", "qsem", "qval")

    def __init__(self, eng, fn, seq):
        self.eng = eng
        self.fn = fn
        self.drain = False
        self.waits_dma = {}
        self.waits_op = []
        self.inc_needed = False
        self.sem_val = None
        self.seq = seq
        self.qsem = None
        self.qval = None


class Buf:
    __slots__ = ("w", "r")

    def __init__(self):
        self.w = None
        self.r = []


class Emitter:
    def __init__(self):
        self.ops = {e: [] for e in ENGINES}
        self.dma_cnt = {q: 0 for q in DMA_QUEUES.values()}
        self.drain_cover = {e: -1 for e in ENGINES}

    def _deps(self, op, reads, writes):
        toks = []
        for b in reads:
            if b.w is not None:
                toks.append(b.w)
        for b in writes:
            if b.w is not None:
                toks.append(b.w)
            toks.extend(b.r)
        for t in toks:
            if t[0] == "dma":
                _, q, v = t
                op.waits_dma[q] = max(op.waits_dma.get(q, 0), v)
            else:
                p = t[1]
                if p.eng == op.eng:
                    if op.eng != "tensor" and p.seq > self.drain_cover[op.eng]:
                        op.drain = True
                else:
                    p.inc_needed = True
                    op.waits_op.append(p)
        if op.drain:
            self.drain_cover[op.eng] = op.seq - 1

    def emit(self, eng, fn, reads=(), writes=()):
        op = Op(eng, fn, len(self.ops[eng]))
        self._deps(op, reads, writes)
        self.ops[eng].append(op)
        tok = ("op", op)
        for b in writes:
            b.w = tok
            b.r = []
        for b in reads:
            b.r.append(tok)
        return op

    def dma(self, eng, out_ap, in_ap, reads=(), writes=()):
        q = DMA_QUEUES[eng]
        op = Op(eng, None, len(self.ops[eng]))
        self._deps(op, reads, writes)
        self.dma_cnt[q] += 16
        op.qsem = q
        op.qval = self.dma_cnt[q]

        def fn(h, out_ap=out_ap, in_ap=in_ap):
            return h.dma_start(out=out_ap, in_=in_ap)
        op.fn = fn
        self.ops[eng].append(op)
        tok = ("dma", q, op.qval)
        for b in writes:
            b.w = tok
            b.r = []
        for b in reads:
            b.r.append(tok)
        return op

    def finalize_and_replay(self, block, sems):
        for e in ENGINES:
            v = 0
            for op in self.ops[e]:
                if op.inc_needed:
                    v += 1
                    op.sem_val = v
        esem = {e: sems["s_" + e] for e in ENGINES}
        qsems = {q: sems[q] for q in self.dma_cnt}

        def body(e):
            ops = self.ops[e]

            def run(h):
                watermark = {}
                for op in ops:
                    targets = {}
                    for q, v in op.waits_dma.items():
                        if watermark.get(q, 0) < v:
                            targets[qsems[q]] = max(targets.get(qsems[q], 0), v)
                            watermark[q] = v
                    for p in op.waits_op:
                        key = "s_" + p.eng
                        if watermark.get(key, 0) < p.sem_val:
                            targets[esem[p.eng]] = max(
                                targets.get(esem[p.eng], 0), p.sem_val)
                            watermark[key] = p.sem_val
                    for s, v in targets.items():
                        h.wait_ge(s, v)
                    if op.drain:
                        h.drain()
                    instr = op.fn(h)
                    if op.qsem is not None:
                        instr.then_inc(qsems[op.qsem], 16)
                    elif op.inc_needed:
                        instr.then_inc(esem[op.eng], 1)
            return run

        for e in ENGINES:
            getattr(block, e)(body(e))


def ceil_div(a, b):
    return (a + b - 1) // b


def build_program():
    nc = bass.Bass("TRN2", target_bir_lowering=False, debug=False)

    dx = nc.dram_tensor("x_in", [S, D], F32, kind="ExternalInput").ap()
    dout = nc.dram_tensor("x_out", [S, D], F32, kind="ExternalOutput").ap()
    dwqT = nc.dram_tensor("wqT", [D, D], F32, kind="ExternalInput").ap()
    dwkT = nc.dram_tensor("wkT", [D, D], F32, kind="ExternalInput").ap()
    dwvT = nc.dram_tensor("wvT", [D, D], F32, kind="ExternalInput").ap()
    dwoT = nc.dram_tensor("woT", [D, D], F32, kind="ExternalInput").ap()
    dw1T = nc.dram_tensor("w1T", [D, HID], F32, kind="ExternalInput").ap()
    dw2T = nc.dram_tensor("w2T", [D, HID], F32, kind="ExternalInput").ap()
    dw3sT = nc.dram_tensor("w3sT", [HID, D], F32, kind="ExternalInput").ap()
    dw12T = nc.dram_tensor("w12T", [NE, D, 2 * RH], F32, kind="ExternalInput").ap()
    dw3eT = nc.dram_tensor("w3eT", [NE, RH, D], F32, kind="ExternalInput").ap()
    drt = nc.dram_tensor("routerT", [D, NE], F32, kind="ExternalInput").ap()
    debias = nc.dram_tensor("ebias", [1, NE], F32, kind="ExternalInput").ap()

    A = nc.alloc_sbuf_tensor
    x_sb = A("x_sb", [128, NT * 1024], F32).ap()
    actT_h = A("actT_h", [128, 8 * TW], BF16).ap()
    actT_l = A("actT_l", [128, 8 * TW], BF16).ap()
    hb_h = A("hb_h", [128, 2 * 1024], BF16).ap()
    hb_l = A("hb_l", [128, 2 * 1024], BF16).ap()
    w3h = A("w3h", [128, 2 * 1024], BF16).ap()
    w3l = A("w3l", [128, 2 * 1024], BF16).ap()
    wsh = A("wsh", [128, 4 * 1024], BF16).ap()
    wsl = A("wsl", [128, 4 * 1024], BF16).ap()
    wf = A("wf", [128, 2048], F32).ap()
    h1s = A("h1s", [128, 1024], F32).ap()
    gst = A("gst", [128, 1024], F32).ap()
    srow = A("srow", [128, S], F32).ap()
    ptb = A("ptb", [128, 2 * 512], F32).ap()
    ot_h = A("ot_h", [128, NT * 128], BF16).ap()
    ot_l = A("ot_l", [128, NT * 128], BF16).ap()
    rs512 = A("rs512", [128, 512], F32).ap()
    mask512 = A("mask512", [128, 128], F32).ap()
    v_sb = wf
    ident = A("ident", [128, 128], F32).ap()
    iotaD = A("iotaD", [128, 512], I32).ap()
    ones1 = A("ones1", [1, 128], F32).ap()
    ocomb = A("ocomb", [128, 128], F32).ap()
    rt_sb = A("rt_sb", [128, 8 * NE], F32).ap()
    biasB = A("biasB", [128, NE], F32).ap()
    bias1 = A("bias1", [1, NE], F32).ap()
    gates05 = A("gates05", [128, NT * NE], F32).ap()
    stats6 = A("stats6", [128, 12], F32).ap()
    stats2 = A("stats2", [128, 2], F32).ap()
    sqv = A("sqv", [128, 1], F32).ap()
    rstd = A("rstd", [128, 1], F32).ap()
    sc_sb = A("sc_sb", [128, NE], F32).ap()
    selb = A("selb", [128, NE], F32).ap()
    tmask = A("tmask", [128, NE], F32).ap()
    masked = A("masked", [128, NE], F32).ap()
    maskf = A("maskf", [128, NE], F32).ap()
    m1c = A("m1c", [128, 1], F32).ap()
    m2c = A("m2c", [128, 1], F32).ap()
    cntc = A("cntc", [128, 1], F32).ap()
    selc = A("selc", [128, 1], F32).ap()
    selc2 = A("selc2", [128, 1], F32).ap()
    mrow = A("mrow", [128, 1], F32).ap()
    negm = A("negm", [128, 1], F32).ap()
    sumr = A("sumr", [128, 1], F32).ap()
    recip = A("recip", [128, 2], F32).ap()

    qth, qtl = hb_h, hb_l
    kth, ktl = w3h, w3l

    psum = nc.alloc_psum_tensor("psum", [128, 4096], F32).ap()

    def bank(i, w=512):
        return psum[:, 512 * i:512 * i + w]

    em = Emitter()

    XB = [Buf() for _ in range(NT)]
    AB = [Buf() for _ in range(NT)]
    ABL = [Buf() for _ in range(NT)]
    PB = [Buf() for _ in range(8)]
    HB = [Buf() for _ in range(2)]
    HBL = [Buf() for _ in range(2)]
    W3C = [Buf() for _ in range(2)]
    W3L = [Buf() for _ in range(2)]
    WS4 = [Buf() for _ in range(4)]
    WS4L = [Buf() for _ in range(4)]
    WFB = [Buf(), Buf()]
    GB = [Buf() for _ in range(NT)]
    PT2 = [Buf() for _ in range(2)]
    STATS = Buf(); SC = Buf(); TOPB = Buf()
    SRB = Buf()
    OTB = Buf(); OTBL = Buf(); OCB = Buf(); RSB = Buf(); H1S = Buf()
    GSTB = Buf()
    CONST = Buf(); RTB = Buf(); BB = Buf(); RECB = Buf()
    QTBL = [HB[0], HB[1], HBL[0], HBL[1]]
    KTBL = [W3C[0], W3C[1], W3L[0], W3L[1]]
    VBL = WFB

    em.emit("gpsimd", lambda h: h.iota(iotaD, pattern=[[1, 512]], base=0,
                                       channel_multiplier=-1), writes=[CONST])
    em.emit("vector", lambda h: h.tensor_scalar(
        ident, iotaD[:, 0:128], 0, None, ALU.is_equal, ALU.bypass),
        reads=[CONST], writes=[CONST])
    rt_view = drt.rearrange("(dc p) e -> p dc e", p=128)
    em.dma("sync", rt_sb.rearrange("p (dc e) -> p dc e", e=NE), rt_view,
           writes=[RTB])
    em.dma("sync", bias1, debias, writes=[BB])
    em.emit("vector", lambda h: h.memset(ones1, 1.0), writes=[CONST])
    em.emit("tensor", lambda h: h.matmul(bank(0, NE), ones1, bias1,
                                         start=True, stop=True),
            reads=[BB, CONST], writes=[PB[0]])
    em.emit("vector", lambda h: h.tensor_copy(biasB, bank(0, NE)),
            reads=[PB[0]], writes=[BB, PB[0]])
    em.emit("vector", lambda h: h.memset(ocomb, 0.0), writes=[OCB])
    em.emit("vector", lambda h: h.tensor_scalar(
        mask512, iotaD[:, 0:128], 0, NEG, ALU.is_gt, ALU.mult),
        reads=[CONST], writes=[CONST])
    xv = dx.rearrange("(t p) d -> p t d", p=128)
    for ti in range(NT):
        em.dma("gpsimd", x_sb[:, 1024 * ti:1024 * (ti + 1)], xv[:, ti, :],
               writes=[XB[ti]])

    athv = actT_h.rearrange("p (dc t) -> p dc t", t=TW)
    atlv = actT_l.rearrange("p (dc t) -> p dc t", t=TW)

    def ln_tile(ti):
        tcol = 128 * ti
        xt = x_sb[:, 1024 * ti:1024 * (ti + 1)]
        em.emit("vector", lambda h: h.bn_stats(stats6[:, 0:6], xt[:, 0:512]),
                reads=[XB[ti]], writes=[STATS])
        em.emit("vector", lambda h: h.bn_stats(stats6[:, 6:12], xt[:, 512:1024]),
                reads=[XB[ti]], writes=[STATS])
        em.emit("vector", lambda h: h.bn_aggr(stats2, stats6),
                reads=[STATS], writes=[STATS])
        em.emit("vector", lambda h: h.tensor_scalar(
            sqv, stats2[:, 1:2], EPS, None, ALU.add, ALU.bypass),
            reads=[STATS], writes=[STATS])
        em.emit("scalar", lambda h: h.activation(out=sqv, in_=sqv,
                                                 func=AF.Sqrt),
                reads=[STATS], writes=[STATS])
        em.emit("vector", lambda h: h.reciprocal(rstd, sqv),
                reads=[STATS], writes=[STATS])
        em.emit("vector", lambda h: h.tensor_scalar(
            h1s, xt, stats2[:, 0:1], rstd, ALU.subtract, ALU.mult),
            reads=[XB[ti], STATS], writes=[H1S])
        for half in range(2):
            b = 6 + half
            for j in range(4):
                dc = 4 * half + j
                em.emit("tensor", lambda h, dc=dc, j=j, b=b: h.transpose(
                    bank(b)[:, 128 * j:128 * (j + 1)],
                    h1s[:, 128 * dc:128 * (dc + 1)], ident),
                    reads=[H1S, CONST], writes=[PB[b]])
            em.emit("vector", lambda h, half=half, b=b: h.tensor_copy(
                gst[:, 512 * half:512 * (half + 1)], bank(b)),
                reads=[PB[b]], writes=[GSTB, PB[b]])
        gv = gst.rearrange("p (dc t) -> p dc t", t=128)
        em.emit("scalar", lambda h, tcol=tcol: h.copy(
            athv[:, 0:8, tcol:tcol + 128], gv),
            reads=[GSTB], writes=[AB[ti]])
        em.emit("vector", lambda h, tcol=tcol: h.tensor_tensor(
            atlv[:, 0:8, tcol:tcol + 128], gv,
            athv[:, 0:8, tcol:tcol + 128], ALU.subtract),
            reads=[GSTB, AB[ti]], writes=[ABL[ti]])

    def router_tile(ti):
        for dc in range(8):
            em.emit("tensor", lambda h, dc=dc: h.matmul(
                bank(5, NE), gst[:, 128 * dc:128 * (dc + 1)],
                rt_sb[:, dc * NE:(dc + 1) * NE],
                start=(dc == 0), stop=(dc == 7)),
                reads=[GSTB, RTB], writes=[PB[5]])
        em.emit("scalar", lambda h: h.activation(out=sc_sb, in_=bank(5, NE),
                                                 func=AF.Sigmoid),
                reads=[PB[5]], writes=[SC, PB[5]])
        em.emit("vector", lambda h: h.tensor_tensor(selb, sc_sb, biasB, ALU.add),
                reads=[SC, BB], writes=[TOPB])
        em.emit("vector", lambda h: h.reduce_max(m1c, selb, axis=AX.X),
                reads=[TOPB], writes=[TOPB])
        em.emit("vector", lambda h: h.tensor_scalar(
            tmask, selb, m1c, None, ALU.is_ge, ALU.bypass),
            reads=[TOPB], writes=[TOPB])
        em.emit("vector", lambda h: h.scalar_tensor_tensor(
            masked, tmask, NEG, selb, ALU.mult, ALU.add),
            reads=[TOPB], writes=[TOPB])
        em.emit("vector", lambda h: h.reduce_max(m2c, masked, axis=AX.X),
                reads=[TOPB], writes=[TOPB])
        em.emit("vector", lambda h: h.tensor_scalar(
            maskf, selb, m2c, None, ALU.is_ge, ALU.bypass),
            reads=[TOPB], writes=[TOPB])
        em.emit("vector", lambda h: h.tensor_reduce(
            cntc, tmask, axis=AX.X, op=ALU.add),
            reads=[TOPB], writes=[TOPB])
        em.emit("vector", lambda h: h.tensor_scalar(
            selc, cntc, 1.5, None, ALU.is_ge, ALU.bypass),
            reads=[TOPB], writes=[TOPB])
        em.emit("vector", lambda h: h.tensor_scalar(
            selc2, selc, -1.0, 1.0, ALU.mult, ALU.add),
            reads=[TOPB], writes=[TOPB])
        em.emit("vector", lambda h: h.tensor_scalar(
            maskf, maskf, selc2, None, ALU.mult, ALU.bypass),
            reads=[TOPB], writes=[TOPB])
        em.emit("vector", lambda h: h.scalar_tensor_tensor(
            maskf, tmask, selc, maskf, ALU.mult, ALU.add),
            reads=[TOPB], writes=[TOPB])
        g = gates05[:, NE * ti:NE * (ti + 1)]
        em.emit("vector", lambda h, g=g: h.scalar_tensor_tensor(
            g, sc_sb, 0.5, maskf, ALU.mult, ALU.mult),
            reads=[SC, TOPB], writes=[GB[ti]])

    wpar = [0]

    def wload_split(dram_ap, wh_ap, wl_ap, wbuf, wbufl):
        par = wpar[0]
        wpar[0] ^= 1
        wfs = wf[:, 1024 * par:1024 * (par + 1)]
        wfd = wfs
        if len(dram_ap.shape) == 3:
            wfd = wfs.rearrange("p (dc c) -> p dc c", c=dram_ap.shape[2])
        em.dma("sync", wfd, dram_ap, writes=[WFB[par]])
        em.emit("scalar", lambda h: h.copy(wh_ap, wfs),
                reads=[WFB[par]], writes=[wbuf])
        em.emit("vector", lambda h: h.tensor_tensor(wl_ap, wfs, wh_ap,
                                                    ALU.subtract),
                reads=[WFB[par], wbuf], writes=[wbufl, WFB[par]])

    def h_stage(j, whb, wlb, c1h, c1l, c2h, c2l, tis):
        toff = 128 * tis[0]
        for ts2 in range(2):
            rdh = [AB[t] for t in tis[4 * ts2:4 * ts2 + 4]]
            rdl = [ABL[t] for t in tis[4 * ts2:4 * ts2 + 4]]
            for wi, (cw, bb) in enumerate(((c1h, 4 + ts2), (c2h, 6 + ts2))):
                cl = c1l if cw is c1h else c2l
                for dc in range(8):
                    ah = actT_h[:, dc * TW + toff + 512 * ts2:
                                dc * TW + toff + 512 * (ts2 + 1)]
                    al = actT_l[:, dc * TW + toff + 512 * ts2:
                                dc * TW + toff + 512 * (ts2 + 1)]
                    for pi, (lw, rw, rb) in enumerate((
                            (cw, ah, [whb[wi]] + rdh),
                            (cw, al, [whb[wi]] + rdl),
                            (cl, ah, [wlb[wi]] + rdh))):
                        em.emit("tensor",
                                lambda h, dc=dc, bb=bb, lw=lw, rw=rw, pi=pi:
                                h.matmul(bank(bb),
                                         lw[:, 128 * dc:128 * (dc + 1)], rw,
                                         start=(dc == 0 and pi == 0),
                                         stop=(dc == 7 and pi == 2)),
                                reads=rb, writes=[PB[bb]])
        em.emit("scalar", lambda h: h.activation(
            out=h1s, in_=psum[:, 2048:3072], func=AF.Silu),
            reads=[PB[4], PB[5]], writes=[H1S, PB[4], PB[5]])
        em.emit("vector", lambda h: h.tensor_tensor(
            gst, psum[:, 3072:4096], h1s, ALU.mult),
            reads=[PB[6], PB[7], H1S], writes=[GSTB, PB[6], PB[7]])
        em.emit("scalar", lambda h, j=j: h.copy(
            hb_h[:, 1024 * j:1024 * (j + 1)], gst),
            reads=[GSTB], writes=[HB[j]])
        em.emit("vector", lambda h, j=j: h.tensor_tensor(
            hb_l[:, 1024 * j:1024 * (j + 1)], gst,
            hb_h[:, 1024 * j:1024 * (j + 1)], ALU.subtract),
            reads=[GSTB, HB[j]], writes=[HBL[j]])

    def out_stage(tis, gcol_fn):
        for k, ti in enumerate(tis):
            for dh in range(2):
                b = (2 * k + dh) % 4
                nmm = 0
                for j in range(2):
                    lh = hb_h[:, 1024 * j + 128 * k:1024 * j + 128 * (k + 1)]
                    ll = hb_l[:, 1024 * j + 128 * k:1024 * j + 128 * (k + 1)]
                    rh_ = w3h[:, 1024 * j + 512 * dh:1024 * j + 512 * (dh + 1)]
                    rl_ = w3l[:, 1024 * j + 512 * dh:1024 * j + 512 * (dh + 1)]
                    for lw, rw, rb in ((lh, rh_, [HB[j], W3C[j]]),
                                       (lh, rl_, [HB[j], W3L[j]]),
                                       (ll, rh_, [HBL[j], W3C[j]])):
                        nmm += 1
                        em.emit("tensor",
                                lambda h, lw=lw, rw=rw, b=b, s=(nmm == 1),
                                e=(nmm == 6):
                                h.matmul(bank(b), lw, rw, start=s, stop=e),
                                reads=rb, writes=[PB[b]])
                gcol, grd = gcol_fn(ti)
                xs = x_sb[:, 1024 * ti + 512 * dh:1024 * ti + 512 * (dh + 1)]
                em.emit("vector", lambda h, b=b, gcol=gcol, xs=xs:
                        h.scalar_tensor_tensor(xs, bank(b), gcol, xs,
                                               ALU.mult, ALU.add),
                        reads=[PB[b], XB[ti]] + grd, writes=[XB[ti], PB[b]])

    def ffn_call():
        for ti in range(NT):
            ln_tile(ti)
            router_tile(ti)
        w1v = dw1T.rearrange("(dc p) hh -> p dc hh", p=128)
        w2v = dw2T.rearrange("(dc p) hh -> p dc hh", p=128)
        for half in range(2):
            tis = list(range(8 * half, 8 * half + 8))
            for g in range(16):
                for j in range(2):
                    hc = 2 * g + j
                    c1h = wsh[:, 1024 * j:1024 * (j + 1)]
                    c1l = wsl[:, 1024 * j:1024 * (j + 1)]
                    c2h = wsh[:, 2048 + 1024 * j:3072 + 1024 * j]
                    c2l = wsl[:, 2048 + 1024 * j:3072 + 1024 * j]
                    wload_split(
                        w1v[:, :, 128 * hc:128 * (hc + 1)], c1h, c1l,
                        WS4[j], WS4L[j])
                    wload_split(
                        w2v[:, :, 128 * hc:128 * (hc + 1)], c2h, c2l,
                        WS4[2 + j], WS4L[2 + j])
                    h_stage(j, (WS4[j], WS4[2 + j]), (WS4L[j], WS4L[2 + j]),
                            c1h, c1l, c2h, c2l, tis)
                    wload_split(dw3sT[128 * hc:128 * (hc + 1), :],
                                w3h[:, 1024 * j:1024 * (j + 1)],
                                w3l[:, 1024 * j:1024 * (j + 1)],
                                W3C[j], W3L[j])
                out_stage(tis, lambda ti: (0.5, []))
            for e in range(NE):
                wv = dw12T[e].rearrange("(dc p) hh -> p dc hh", p=128)
                for g2 in range(4):
                    for j in range(2):
                        rhc = 2 * g2 + j
                        c1h = wsh[:, 2048 * j:2048 * j + 1024]
                        c1l = wsl[:, 2048 * j:2048 * j + 1024]
                        c2h = wsh[:, 2048 * j + 1024:2048 * (j + 1)]
                        c2l = wsl[:, 2048 * j + 1024:2048 * (j + 1)]
                        wload_split(
                            wv[:, :, 128 * rhc:128 * (rhc + 1)],
                            c1h, c1l, WS4[2 * j], WS4L[2 * j])
                        wload_split(
                            wv[:, :, RH + 128 * rhc:RH + 128 * (rhc + 1)],
                            c2h, c2l, WS4[2 * j + 1], WS4L[2 * j + 1])
                        h_stage(j, (WS4[2 * j], WS4[2 * j + 1]),
                                (WS4L[2 * j], WS4L[2 * j + 1]),
                                c1h, c1l, c2h, c2l, tis)
                        wload_split(dw3eT[e][128 * rhc:128 * (rhc + 1), :],
                                    w3h[:, 1024 * j:1024 * (j + 1)],
                                    w3l[:, 1024 * j:1024 * (j + 1)],
                                    W3C[j], W3L[j])
                    out_stage(tis, lambda ti, e=e: (
                        gates05[:, NE * ti + e:NE * ti + e + 1], [GB[ti]]))

    def attn_block():
        for ti in range(NT):
            ln_tile(ti)
        slopes = [2.0 ** (-(hh + 1) / 2.0) for hh in range(HEADS)]
        for p in range(8):
            wslices = []
            for qi_, dw in enumerate((dwqT, dwkT, dwvT)):
                wvw = dw.rearrange("(dc p2) o -> p2 dc o", p2=128)
                ch = wsh[:, 1024 * qi_:1024 * (qi_ + 1)]
                cl = wsl[:, 1024 * qi_:1024 * (qi_ + 1)]
                wload_split(
                    wvw[:, :, 128 * p:128 * (p + 1)], ch, cl,
                    WS4[qi_], WS4L[qi_])
                wslices.append((ch, cl))
            woh = wsh[:, 3072:4096]
            wol = wsl[:, 3072:4096]
            wload_split(dwoT[128 * p:128 * (p + 1), :], woh, wol,
                        WS4[3], WS4L[3])
            (wqh, wql), (wkh, wkl), (wvh, wvl) = wslices
            for ts4 in range(4):
                rdh = [AB[t] for t in range(4 * ts4, 4 * ts4 + 4)]
                rdl = [ABL[t] for t in range(4 * ts4, 4 * ts4 + 4)]
                for wi, (dsth, dstl, hbl, lbl, boff) in enumerate(
                        ((qth, qtl, [HB[0], HB[1]], [HBL[0], HBL[1]], 4),
                         (kth, ktl, [W3C[0], W3C[1]], [W3L[0], W3L[1]], 6))):
                    wh_ = wsh[:, 1024 * wi:1024 * (wi + 1)]
                    wl_ = wsl[:, 1024 * wi:1024 * (wi + 1)]
                    b = boff + ts4 % 2
                    for dc in range(8):
                        ah = actT_h[:, dc * TW + 512 * ts4:
                                    dc * TW + 512 * (ts4 + 1)]
                        al = actT_l[:, dc * TW + 512 * ts4:
                                    dc * TW + 512 * (ts4 + 1)]
                        for pi, (lw, rw, rb) in enumerate(
                                ((wh_, ah, [WS4[wi]] + rdh),
                                 (wh_, al, [WS4[wi]] + rdl),
                                 (wl_, ah, [WS4L[wi]] + rdh))):
                            em.emit("tensor",
                                    lambda h, dc=dc, b=b, lw=lw, rw=rw, pi=pi:
                                    h.matmul(bank(b),
                                             lw[:, 128 * dc:128 * (dc + 1)],
                                             rw,
                                             start=(dc == 0 and pi == 0),
                                             stop=(dc == 7 and pi == 2)),
                                    reads=rb, writes=[PB[b]])
                    em.emit("scalar", lambda h, dsth=dsth, ts4=ts4, b=b:
                            h.copy(dsth[:, 512 * ts4:512 * (ts4 + 1)],
                                   bank(b)),
                            reads=[PB[b]], writes=list(hbl))
                    em.emit("vector",
                            lambda h, dsth=dsth, dstl=dstl, ts4=ts4, b=b:
                            h.tensor_tensor(dstl[:, 512 * ts4:512 * (ts4 + 1)],
                                            bank(b),
                                            dsth[:, 512 * ts4:512 * (ts4 + 1)],
                                            ALU.subtract),
                            reads=[PB[b]] + list(hbl),
                            writes=list(lbl) + [PB[b]])
            for k in range(NT):
                b = k % 2
                for dc in range(8):
                    ah = athv[:, dc, 128 * k:128 * (k + 1)]
                    al = atlv[:, dc, 128 * k:128 * (k + 1)]
                    vh = wvh[:, 128 * dc:128 * (dc + 1)]
                    vl = wvl[:, 128 * dc:128 * (dc + 1)]
                    for pi, (lw, rw, rb) in enumerate(
                            ((ah, vh, [WS4[2], AB[k]]),
                             (ah, vl, [WS4L[2], AB[k]]),
                             (al, vh, [WS4[2], ABL[k]]))):
                        em.emit("tensor",
                                lambda h, b=b, lw=lw, rw=rw, dc=dc, pi=pi:
                                h.matmul(bank(b, 128), lw, rw,
                                         start=(dc == 0 and pi == 0),
                                         stop=(dc == 7 and pi == 2)),
                                reads=rb, writes=[PB[b]])
                em.emit("scalar", lambda h, k=k, b=b: h.copy(
                    v_sb[:, 128 * k:128 * (k + 1)], bank(b, 128)),
                    reads=[PB[b]], writes=[VBL[k // 8], PB[b]])
            for hi in range(2):
                hof = 64 * hi
                slope = slopes[2 * p + hi]
                em.emit("vector", lambda h, slope=slope: h.tensor_scalar(
                    rs512, iotaD, slope, None, ALU.mult, ALU.bypass),
                    reads=[CONST], writes=[RSB])
                for qi in range(NT):
                    kmax = 128 * (qi + 1)
                    nkj = ceil_div(kmax, 512)
                    for kj in range(nkj):
                        w = min(512, kmax - 512 * kj)
                        b = 4 + (kj % 2)
                        qh_ = qth[hof:hof + 64, 128 * qi:128 * (qi + 1)]
                        ql_ = qtl[hof:hof + 64, 128 * qi:128 * (qi + 1)]
                        kh_ = kth[hof:hof + 64, 512 * kj:512 * kj + w]
                        kl_ = ktl[hof:hof + 64, 512 * kj:512 * kj + w]
                        for pi, (lw, rw) in enumerate(
                                ((qh_, kh_), (qh_, kl_), (ql_, kh_))):
                            em.emit("tensor",
                                    lambda h, w=w, b=b, lw=lw, rw=rw, pi=pi:
                                    h.matmul(bank(b, w), lw, rw,
                                             start=(pi == 0), stop=(pi == 2)),
                                    reads=QTBL + KTBL, writes=[PB[b]])
                        em.emit("vector", lambda h, kj=kj, w=w, b=b, s=slope:
                                h.tensor_scalar(
                                    srow[:, 512 * kj:512 * kj + w],
                                    bank(b, w), 0.125, s * 512.0 * kj,
                                    ALU.mult, ALU.add),
                                reads=[PB[b]], writes=[SRB, PB[b]])
                        em.emit("vector", lambda h, kj=kj, w=w:
                                h.tensor_tensor(
                                    srow[:, 512 * kj:512 * kj + w],
                                    srow[:, 512 * kj:512 * kj + w],
                                    rs512[:, 0:w], ALU.add),
                                reads=[SRB, RSB], writes=[SRB])
                    em.emit("vector", lambda h, kmax=kmax: h.tensor_tensor(
                        srow[:, kmax - 128:kmax], srow[:, kmax - 128:kmax],
                        mask512, ALU.add),
                        reads=[SRB, CONST], writes=[SRB])
                    em.emit("vector", lambda h, kmax=kmax: h.reduce_max(
                        mrow, srow[:, 0:kmax], axis=AX.X),
                        reads=[SRB], writes=[STATS])
                    em.emit("vector", lambda h: h.tensor_scalar(
                        negm, mrow, -1.0, None, ALU.mult, ALU.bypass),
                        reads=[STATS], writes=[STATS])
                    em.emit("scalar", lambda h, kmax=kmax: h.activation(
                        out=srow[:, 0:kmax], in_=srow[:, 0:kmax], func=AF.Exp,
                        bias=negm, scale=1.0, accum_out=sumr),
                        reads=[SRB, STATS], writes=[SRB, STATS])
                    em.emit("vector", lambda h, hi=hi: h.reciprocal(
                        recip[:, hi:hi + 1], sumr),
                        reads=[STATS], writes=[RECB])
                    nkt = kmax // 128
                    for kt4 in range(ceil_div(nkt, 4)):
                        b = 6 + (kt4 % 2)
                        n = min(4, nkt - 4 * kt4)
                        sl = PT2[kt4 % 2]
                        for j in range(n):
                            kt = 4 * kt4 + j
                            em.emit("tensor", lambda h, kt=kt, j=j, b=b:
                                    h.transpose(
                                        bank(b)[:, 128 * j:128 * (j + 1)],
                                        srow[:, 128 * kt:128 * (kt + 1)],
                                        ident),
                                    reads=[SRB, CONST], writes=[PB[b]])
                        em.emit("vector", lambda h, kt4=kt4, b=b, n=n:
                                h.tensor_copy(
                                    ptb[:, 512 * (kt4 % 2):
                                        512 * (kt4 % 2) + 128 * n],
                                    bank(b, 128 * n)),
                                reads=[PB[b]], writes=[sl, PB[b]])
                        for j in range(n):
                            kt = 4 * kt4 + j
                            em.emit("tensor",
                                    lambda h, kt=kt, j=j, kt4=kt4, hof=hof,
                                    hi=hi, nkt=nkt:
                                    h.matmul(
                                        bank(3, 128)[:, 64 * hi:64 * (hi + 1)],
                                        ptb[:, 512 * (kt4 % 2) + 128 * j:
                                            512 * (kt4 % 2) + 128 * (j + 1)],
                                        v_sb[:, 128 * kt + hof:
                                             128 * kt + hof + 64],
                                        start=(kt == 0), stop=(kt == nkt - 1)),
                                    reads=[sl, VBL[kt // 8]], writes=[PB[3]])
                    em.emit("vector", lambda h, hi=hi: h.tensor_scalar(
                        ocomb[:, 64 * hi:64 * (hi + 1)],
                        bank(3, 128)[:, 64 * hi:64 * (hi + 1)],
                        recip[:, hi:hi + 1], None, ALU.mult, ALU.bypass),
                        reads=[PB[3], RECB], writes=[OCB, PB[3]])
                    em.emit("tensor", lambda h: h.transpose(bank(2, 128),
                                                            ocomb, ident),
                            reads=[OCB, CONST], writes=[PB[2]])
                    em.emit("scalar", lambda h, qi=qi, hof=hof: h.copy(
                        ot_h[hof:hof + 64, 128 * qi:128 * (qi + 1)],
                        bank(2, 128)[hof:hof + 64, :]),
                        reads=[PB[2]], writes=[OTB])
                    em.emit("vector", lambda h, qi=qi, hof=hof:
                            h.tensor_tensor(
                                ot_l[hof:hof + 64, 128 * qi:128 * (qi + 1)],
                                bank(2, 128)[hof:hof + 64, :],
                                ot_h[hof:hof + 64, 128 * qi:128 * (qi + 1)],
                                ALU.subtract),
                            reads=[PB[2], OTB], writes=[OTBL, PB[2]])
            for k in range(NT):
                for dh in range(2):
                    b = (2 * k + dh) % 2
                    oh_ = ot_h[:, 128 * k:128 * (k + 1)]
                    ol_ = ot_l[:, 128 * k:128 * (k + 1)]
                    rh_ = woh[:, 512 * dh:512 * (dh + 1)]
                    rl_ = wol[:, 512 * dh:512 * (dh + 1)]
                    for pi, (lw, rw, rb) in enumerate(
                            ((oh_, rh_, [OTB, WS4[3]]),
                             (oh_, rl_, [OTB, WS4L[3]]),
                             (ol_, rh_, [OTBL, WS4[3]]))):
                        em.emit("tensor",
                                lambda h, b=b, lw=lw, rw=rw, pi=pi:
                                h.matmul(bank(b), lw, rw,
                                         start=(pi == 0), stop=(pi == 2)),
                                reads=rb, writes=[PB[b]])
                    xs = x_sb[:, 1024 * k + 512 * dh:1024 * k + 512 * (dh + 1)]
                    em.emit("vector", lambda h, b=b, xs=xs:
                            h.scalar_tensor_tensor(xs, bank(b), 1.0, xs,
                                                   ALU.mult, ALU.add),
                            reads=[PB[b], XB[k]], writes=[XB[k], PB[b]])

    ffn_call()
    attn_block()
    ffn_call()
    ov = dout.rearrange("(t p) d -> p t d", p=128)
    for ti in range(NT):
        em.dma("gpsimd", ov[:, ti, :], x_sb[:, 1024 * ti:1024 * (ti + 1)],
               reads=[XB[ti]])
    final_q = em.dma_cnt["qG"]

    stack = contextlib.ExitStack()
    sems = {}
    for name in ["s_tensor", "s_vector", "s_scalar", "s_gpsimd", "s_sync",
                 "qS", "qG", "qA"]:
        sems[name] = stack.enter_context(nc.semaphore(name))
    block = stack.enter_context(nc.Block())
    fin = Op("gpsimd", lambda h: h.nop(), len(em.ops["gpsimd"]))
    fin.waits_dma["qG"] = final_q
    em.ops["gpsimd"].append(fin)
    em.finalize_and_replay(block, sems)
    stack.close()

    nc.m.detect_race_conditions = False
    return nc


_CACHE = {}


def _get_nc():
    if "full" not in _CACHE:
        _CACHE["full"] = build_program()
    return _CACHE["full"]


def kernel(L0_attn_x, L0_attn_W_Q, L0_attn_W_K, L0_attn_W_V, L0_attn_W_O,
           L0_ffn_w12, L0_ffn_w3, L0_ffn_shared_w1, L0_ffn_shared_w2,
           L0_ffn_shared_w3, L0_ffn_router_w, L0_ffn_expert_bias):
    x = np.ascontiguousarray(np.asarray(L0_attn_x, dtype=np.float32))

    def T(a):
        return np.ascontiguousarray(np.asarray(a, dtype=np.float32).T)

    w = {
        "wqT": T(L0_attn_W_Q), "wkT": T(L0_attn_W_K), "wvT": T(L0_attn_W_V),
        "woT": T(L0_attn_W_O),
        "w1T": T(L0_ffn_shared_w1), "w2T": T(L0_ffn_shared_w2),
        "w3sT": T(L0_ffn_shared_w3),
        "w12T": np.ascontiguousarray(
            np.asarray(L0_ffn_w12, dtype=np.float32).transpose(0, 2, 1)),
        "w3eT": np.ascontiguousarray(
            np.asarray(L0_ffn_w3, dtype=np.float32).transpose(0, 2, 1)),
        "routerT": T(L0_ffn_router_w),
        "ebias": np.ascontiguousarray(
            np.asarray(L0_ffn_expert_bias, dtype=np.float32).reshape(1, NE)),
    }
    nc = _get_nc()
    in_maps = [dict(w, x_in=np.ascontiguousarray(x[b])) for b in range(NCORES)]
    res = run_bass_kernel_spmd(nc, in_maps, core_ids=list(range(NCORES)))
    out = np.stack([res.results[b]["x_out"] for b in range(NCORES)], axis=0)
    return out.astype(np.float32)


# version 05
# speedup vs baseline: 1.2007x; 1.0281x over previous
import os
os.environ.setdefault("BASS_DISABLE_FRAME_TO_TRACEBACK", "1")

import contextlib
import numpy as np

import concourse.bass as bass
import concourse.mybir as mybir
from concourse.bass_utils import run_bass_kernel_spmd

F32 = mybir.dt.float32
BF16 = mybir.dt.bfloat16
I32 = mybir.dt.int32
AF = mybir.ActivationFunctionType
ALU = mybir.AluOpType
AX = mybir.AxisListType

NCORES = 8
S = 2048
D = 1024
NT = S // 128
HEADS = 16
HID = 4096
RH = 1024
NE = 8
EPS = 1e-5
NEG = -1.0e30
TW = 2048

ENGINES = ("tensor", "vector", "scalar", "gpsimd", "sync")
DMA_QUEUES = {"sync": "qS", "gpsimd": "qG", "scalar": "qA"}


class Op:
    __slots__ = ("eng", "fn", "drain", "waits_dma", "waits_op", "inc_needed",
                 "sem_val", "seq", "qsem", "qval")

    def __init__(self, eng, fn, seq):
        self.eng = eng
        self.fn = fn
        self.drain = False
        self.waits_dma = {}
        self.waits_op = []
        self.inc_needed = False
        self.sem_val = None
        self.seq = seq
        self.qsem = None
        self.qval = None


class Buf:
    __slots__ = ("w", "r")

    def __init__(self):
        self.w = None
        self.r = []


class Emitter:
    def __init__(self):
        self.ops = {e: [] for e in ENGINES}
        self.dma_cnt = {q: 0 for q in DMA_QUEUES.values()}
        self.drain_cover = {e: -1 for e in ENGINES}

    def _deps(self, op, reads, writes):
        toks = []
        for b in reads:
            if b.w is not None:
                toks.append(b.w)
        for b in writes:
            if b.w is not None:
                toks.append(b.w)
            toks.extend(b.r)
        for t in toks:
            if t[0] == "dma":
                _, q, v = t
                op.waits_dma[q] = max(op.waits_dma.get(q, 0), v)
            else:
                p = t[1]
                if p.eng == op.eng:
                    if op.eng != "tensor" and p.seq > self.drain_cover[op.eng]:
                        op.drain = True
                else:
                    p.inc_needed = True
                    op.waits_op.append(p)
        if op.drain:
            self.drain_cover[op.eng] = op.seq - 1

    def emit(self, eng, fn, reads=(), writes=()):
        op = Op(eng, fn, len(self.ops[eng]))
        self._deps(op, reads, writes)
        self.ops[eng].append(op)
        tok = ("op", op)
        for b in writes:
            b.w = tok
            b.r = []
        for b in reads:
            b.r.append(tok)
        return op

    def dma(self, eng, out_ap, in_ap, reads=(), writes=()):
        q = DMA_QUEUES[eng]
        op = Op(eng, None, len(self.ops[eng]))
        self._deps(op, reads, writes)
        self.dma_cnt[q] += 16
        op.qsem = q
        op.qval = self.dma_cnt[q]

        def fn(h, out_ap=out_ap, in_ap=in_ap):
            return h.dma_start(out=out_ap, in_=in_ap)
        op.fn = fn
        self.ops[eng].append(op)
        tok = ("dma", q, op.qval)
        for b in writes:
            b.w = tok
            b.r = []
        for b in reads:
            b.r.append(tok)
        return op

    def finalize_and_replay(self, block, sems):
        for e in ENGINES:
            v = 0
            for op in self.ops[e]:
                if op.inc_needed:
                    v += 1
                    op.sem_val = v
        esem = {e: sems["s_" + e] for e in ENGINES}
        qsems = {q: sems[q] for q in self.dma_cnt}

        def body(e):
            ops = self.ops[e]

            def run(h):
                watermark = {}
                for op in ops:
                    targets = {}
                    for q, v in op.waits_dma.items():
                        if watermark.get(q, 0) < v:
                            targets[qsems[q]] = max(targets.get(qsems[q], 0), v)
                            watermark[q] = v
                    for p in op.waits_op:
                        key = "s_" + p.eng
                        if watermark.get(key, 0) < p.sem_val:
                            targets[esem[p.eng]] = max(
                                targets.get(esem[p.eng], 0), p.sem_val)
                            watermark[key] = p.sem_val
                    for s, v in targets.items():
                        h.wait_ge(s, v)
                    if op.drain:
                        h.drain()
                    instr = op.fn(h)
                    if op.qsem is not None:
                        instr.then_inc(qsems[op.qsem], 16)
                    elif op.inc_needed:
                        instr.then_inc(esem[op.eng], 1)
            return run

        for e in ENGINES:
            getattr(block, e)(body(e))


def ceil_div(a, b):
    return (a + b - 1) // b


def build_program():
    nc = bass.Bass("TRN2", target_bir_lowering=False, debug=False)

    dx = nc.dram_tensor("x_in", [S, D], F32, kind="ExternalInput").ap()
    dout = nc.dram_tensor("x_out", [S, D], F32, kind="ExternalOutput").ap()
    dwqT = nc.dram_tensor("wqT", [D, D], F32, kind="ExternalInput").ap()
    dwkT = nc.dram_tensor("wkT", [D, D], F32, kind="ExternalInput").ap()
    dwvT = nc.dram_tensor("wvT", [D, D], F32, kind="ExternalInput").ap()
    dwoT = nc.dram_tensor("woT", [D, D], F32, kind="ExternalInput").ap()
    dw1T = nc.dram_tensor("w1T", [D, HID], F32, kind="ExternalInput").ap()
    dw2T = nc.dram_tensor("w2T", [D, HID], F32, kind="ExternalInput").ap()
    dw3sT = nc.dram_tensor("w3sT", [HID, D], F32, kind="ExternalInput").ap()
    dw12T = nc.dram_tensor("w12T", [NE, D, 2 * RH], F32, kind="ExternalInput").ap()
    dw3eT = nc.dram_tensor("w3eT", [NE, RH, D], F32, kind="ExternalInput").ap()
    drt = nc.dram_tensor("routerT", [D, NE], F32, kind="ExternalInput").ap()
    debias = nc.dram_tensor("ebias", [1, NE], F32, kind="ExternalInput").ap()

    A = nc.alloc_sbuf_tensor
    x_sb = A("x_sb", [128, NT * 1024], F32).ap()
    actT_h = A("actT_h", [128, 8 * TW], BF16).ap()
    actT_l = A("actT_l", [128, 8 * TW], BF16).ap()
    hb_h = A("hb_h", [128, 2 * 1024], BF16).ap()
    hb_l = A("hb_l", [128, 2 * 1024], BF16).ap()
    w3h = A("w3h", [128, 2 * 1024], BF16).ap()
    w3l = A("w3l", [128, 2 * 1024], BF16).ap()
    wsh = A("wsh", [128, 4 * 1024], BF16).ap()
    wsl = A("wsl", [128, 4 * 1024], BF16).ap()
    wf = A("wf", [128, 2048], F32).ap()
    h1s = A("h1s", [128, 1024], F32).ap()
    gst = A("gst", [128, 1024], F32).ap()
    srow = A("srow", [128, S], F32).ap()
    ptb = A("ptb", [128, 2 * 512], F32).ap()
    srow_b = A("srow_b", [128, S], F32).ap()
    otc_h = A("otc_h", [128, 128], BF16).ap()
    otc_l = A("otc_l", [128, 128], BF16).ap()
    mch = A("mch", [128, 8], F32).ap()
    sch = A("sch", [128, 4], F32).ap()
    rs512 = A("rs512", [128, 512], F32).ap()
    mask512 = A("mask512", [128, 128], F32).ap()
    v_sb = wf
    ident = A("ident", [128, 128], F32).ap()
    iotaD = A("iotaD", [128, 512], I32).ap()
    ones1 = A("ones1", [1, 128], F32).ap()
    ocomb = A("ocomb", [128, 128], F32).ap()
    rt_sb = A("rt_sb", [128, 8 * NE], F32).ap()
    biasB = A("biasB", [128, NE], F32).ap()
    bias1 = A("bias1", [1, NE], F32).ap()
    gates05 = A("gates05", [128, NT * NE], F32).ap()
    stats6 = A("stats6", [128, 12], F32).ap()
    stats2 = A("stats2", [128, 2], F32).ap()
    sqv = A("sqv", [128, 1], F32).ap()
    rstd = A("rstd", [128, 1], F32).ap()
    sc_sb = A("sc_sb", [128, NE], F32).ap()
    selb = A("selb", [128, NE], F32).ap()
    tmask = A("tmask", [128, NE], F32).ap()
    masked = A("masked", [128, NE], F32).ap()
    maskf = A("maskf", [128, NE], F32).ap()
    m1c = A("m1c", [128, 1], F32).ap()
    m2c = A("m2c", [128, 1], F32).ap()
    cntc = A("cntc", [128, 1], F32).ap()
    selc = A("selc", [128, 1], F32).ap()
    selc2 = A("selc2", [128, 1], F32).ap()
    mrow = A("mrow", [128, 1], F32).ap()
    negm = A("negm", [128, 1], F32).ap()
    negm2 = A("negm2", [128, 2], F32).ap()
    sumr = A("sumr", [128, 1], F32).ap()
    recip = A("recip", [128, 2], F32).ap()

    qth, qtl = hb_h, hb_l
    kth, ktl = w3h, w3l

    psum = nc.alloc_psum_tensor("psum", [128, 4096], F32).ap()

    def bank(i, w=512):
        return psum[:, 512 * i:512 * i + w]

    em = Emitter()

    XB = [Buf() for _ in range(NT)]
    AB = [Buf() for _ in range(NT)]
    ABL = [Buf() for _ in range(NT)]
    PB = [Buf() for _ in range(8)]
    HB = [Buf() for _ in range(2)]
    HBL = [Buf() for _ in range(2)]
    W3C = [Buf() for _ in range(2)]
    W3L = [Buf() for _ in range(2)]
    WS4 = [Buf() for _ in range(4)]
    WS4L = [Buf() for _ in range(4)]
    WFB = [Buf(), Buf()]
    GB = [Buf() for _ in range(NT)]
    PT2 = [Buf() for _ in range(2)]
    STATS = Buf(); SC = Buf(); TOPB = Buf()
    SRB = Buf()
    OTB = Buf(); OTBL = Buf(); OCB = Buf(); RSB = Buf(); H1S = Buf()
    GSTB = Buf()
    SRB2 = [Buf(), Buf()]
    STQ = [Buf(), Buf()]
    SCHB = Buf()
    CONST = Buf(); RTB = Buf(); BB = Buf(); RECB = Buf()
    QTBL = [HB[0], HB[1], HBL[0], HBL[1]]
    KTBL = [W3C[0], W3C[1], W3L[0], W3L[1]]
    VBL = WFB

    em.emit("gpsimd", lambda h: h.iota(iotaD, pattern=[[1, 512]], base=0,
                                       channel_multiplier=-1), writes=[CONST])
    em.emit("vector", lambda h: h.tensor_scalar(
        ident, iotaD[:, 0:128], 0, None, ALU.is_equal, ALU.bypass),
        reads=[CONST], writes=[CONST])
    rt_view = drt.rearrange("(dc p) e -> p dc e", p=128)
    em.dma("sync", rt_sb.rearrange("p (dc e) -> p dc e", e=NE), rt_view,
           writes=[RTB])
    em.dma("sync", bias1, debias, writes=[BB])
    em.emit("vector", lambda h: h.memset(ones1, 1.0), writes=[CONST])
    em.emit("tensor", lambda h: h.matmul(bank(0, NE), ones1, bias1,
                                         start=True, stop=True),
            reads=[BB, CONST], writes=[PB[0]])
    em.emit("vector", lambda h: h.tensor_copy(biasB, bank(0, NE)),
            reads=[PB[0]], writes=[BB, PB[0]])
    em.emit("vector", lambda h: h.memset(ocomb, 0.0), writes=[OCB])
    em.emit("vector", lambda h: h.tensor_scalar(
        mask512, iotaD[:, 0:128], 0, NEG, ALU.is_gt, ALU.mult),
        reads=[CONST], writes=[CONST])
    xv = dx.rearrange("(t p) d -> p t d", p=128)
    for ti in range(NT):
        em.dma("gpsimd", x_sb[:, 1024 * ti:1024 * (ti + 1)], xv[:, ti, :],
               writes=[XB[ti]])

    athv = actT_h.rearrange("p (dc t) -> p dc t", t=TW)
    atlv = actT_l.rearrange("p (dc t) -> p dc t", t=TW)

    def ln_tile(ti):
        tcol = 128 * ti
        xt = x_sb[:, 1024 * ti:1024 * (ti + 1)]
        em.emit("vector", lambda h: h.bn_stats(stats6[:, 0:6], xt[:, 0:512]),
                reads=[XB[ti]], writes=[STATS])
        em.emit("vector", lambda h: h.bn_stats(stats6[:, 6:12], xt[:, 512:1024]),
                reads=[XB[ti]], writes=[STATS])
        em.emit("vector", lambda h: h.bn_aggr(stats2, stats6),
                reads=[STATS], writes=[STATS])
        em.emit("vector", lambda h: h.tensor_scalar(
            sqv, stats2[:, 1:2], EPS, None, ALU.add, ALU.bypass),
            reads=[STATS], writes=[STATS])
        em.emit("scalar", lambda h: h.activation(out=sqv, in_=sqv,
                                                 func=AF.Sqrt),
                reads=[STATS], writes=[STATS])
        em.emit("vector", lambda h: h.reciprocal(rstd, sqv),
                reads=[STATS], writes=[STATS])
        em.emit("vector", lambda h: h.tensor_scalar(
            h1s, xt, stats2[:, 0:1], rstd, ALU.subtract, ALU.mult),
            reads=[XB[ti], STATS], writes=[H1S])
        for half in range(2):
            b = 6 + half
            for j in range(4):
                dc = 4 * half + j
                em.emit("tensor", lambda h, dc=dc, j=j, b=b: h.transpose(
                    bank(b)[:, 128 * j:128 * (j + 1)],
                    h1s[:, 128 * dc:128 * (dc + 1)], ident),
                    reads=[H1S, CONST], writes=[PB[b]])
            em.emit("vector", lambda h, half=half, b=b: h.tensor_copy(
                gst[:, 512 * half:512 * (half + 1)], bank(b)),
                reads=[PB[b]], writes=[GSTB, PB[b]])
        gv = gst.rearrange("p (dc t) -> p dc t", t=128)
        em.emit("scalar", lambda h, tcol=tcol: h.copy(
            athv[:, 0:8, tcol:tcol + 128], gv),
            reads=[GSTB], writes=[AB[ti]])
        em.emit("vector", lambda h, tcol=tcol: h.tensor_tensor(
            atlv[:, 0:8, tcol:tcol + 128], gv,
            athv[:, 0:8, tcol:tcol + 128], ALU.subtract),
            reads=[GSTB, AB[ti]], writes=[ABL[ti]])

    def router_tile(ti):
        for dc in range(8):
            em.emit("tensor", lambda h, dc=dc: h.matmul(
                bank(5, NE), gst[:, 128 * dc:128 * (dc + 1)],
                rt_sb[:, dc * NE:(dc + 1) * NE],
                start=(dc == 0), stop=(dc == 7)),
                reads=[GSTB, RTB], writes=[PB[5]])
        em.emit("scalar", lambda h: h.activation(out=sc_sb, in_=bank(5, NE),
                                                 func=AF.Sigmoid),
                reads=[PB[5]], writes=[SC, PB[5]])
        em.emit("vector", lambda h: h.tensor_tensor(selb, sc_sb, biasB, ALU.add),
                reads=[SC, BB], writes=[TOPB])
        em.emit("vector", lambda h: h.reduce_max(m1c, selb, axis=AX.X),
                reads=[TOPB], writes=[TOPB])
        em.emit("vector", lambda h: h.tensor_scalar(
            tmask, selb, m1c, None, ALU.is_ge, ALU.bypass),
            reads=[TOPB], writes=[TOPB])
        em.emit("vector", lambda h: h.scalar_tensor_tensor(
            masked, tmask, NEG, selb, ALU.mult, ALU.add),
            reads=[TOPB], writes=[TOPB])
        em.emit("vector", lambda h: h.reduce_max(m2c, masked, axis=AX.X),
                reads=[TOPB], writes=[TOPB])
        em.emit("vector", lambda h: h.tensor_scalar(
            maskf, selb, m2c, None, ALU.is_ge, ALU.bypass),
            reads=[TOPB], writes=[TOPB])
        em.emit("vector", lambda h: h.tensor_reduce(
            cntc, tmask, axis=AX.X, op=ALU.add),
            reads=[TOPB], writes=[TOPB])
        em.emit("vector", lambda h: h.tensor_scalar(
            selc, cntc, 1.5, None, ALU.is_ge, ALU.bypass),
            reads=[TOPB], writes=[TOPB])
        em.emit("vector", lambda h: h.tensor_scalar(
            selc2, selc, -1.0, 1.0, ALU.mult, ALU.add),
            reads=[TOPB], writes=[TOPB])
        em.emit("vector", lambda h: h.tensor_scalar(
            maskf, maskf, selc2, None, ALU.mult, ALU.bypass),
            reads=[TOPB], writes=[TOPB])
        em.emit("vector", lambda h: h.scalar_tensor_tensor(
            maskf, tmask, selc, maskf, ALU.mult, ALU.add),
            reads=[TOPB], writes=[TOPB])
        g = gates05[:, NE * ti:NE * (ti + 1)]
        em.emit("vector", lambda h, g=g: h.scalar_tensor_tensor(
            g, sc_sb, 0.5, maskf, ALU.mult, ALU.mult),
            reads=[SC, TOPB], writes=[GB[ti]])

    wpar = [0]

    def wload_split(dram_ap, wh_ap, wl_ap, wbuf, wbufl):
        par = wpar[0]
        wpar[0] ^= 1
        wfs = wf[:, 1024 * par:1024 * (par + 1)]
        wfd = wfs
        if len(dram_ap.shape) == 3:
            wfd = wfs.rearrange("p (dc c) -> p dc c", c=dram_ap.shape[2])
        em.dma("sync", wfd, dram_ap, writes=[WFB[par]])
        em.emit("scalar", lambda h: h.copy(wh_ap, wfs),
                reads=[WFB[par]], writes=[wbuf])
        em.emit("vector", lambda h: h.tensor_tensor(wl_ap, wfs, wh_ap,
                                                    ALU.subtract),
                reads=[WFB[par], wbuf], writes=[wbufl, WFB[par]])

    def h_stage(j, whb, wlb, c1h, c1l, c2h, c2l, tis):
        toff = 128 * tis[0]
        for ts2 in range(2):
            rdh = [AB[t] for t in tis[4 * ts2:4 * ts2 + 4]]
            rdl = [ABL[t] for t in tis[4 * ts2:4 * ts2 + 4]]
            for wi, (cw, bb) in enumerate(((c1h, 4 + ts2), (c2h, 6 + ts2))):
                cl = c1l if cw is c1h else c2l
                for dc in range(8):
                    ah = actT_h[:, dc * TW + toff + 512 * ts2:
                                dc * TW + toff + 512 * (ts2 + 1)]
                    al = actT_l[:, dc * TW + toff + 512 * ts2:
                                dc * TW + toff + 512 * (ts2 + 1)]
                    for pi, (lw, rw, rb) in enumerate((
                            (cw, ah, [whb[wi]] + rdh),
                            (cw, al, [whb[wi]] + rdl),
                            (cl, ah, [wlb[wi]] + rdh))):
                        em.emit("tensor",
                                lambda h, dc=dc, bb=bb, lw=lw, rw=rw, pi=pi:
                                h.matmul(bank(bb),
                                         lw[:, 128 * dc:128 * (dc + 1)], rw,
                                         start=(dc == 0 and pi == 0),
                                         stop=(dc == 7 and pi == 2)),
                                reads=rb, writes=[PB[bb]])
        em.emit("scalar", lambda h: h.activation(
            out=h1s, in_=psum[:, 2048:3072], func=AF.Silu),
            reads=[PB[4], PB[5]], writes=[H1S, PB[4], PB[5]])
        em.emit("vector", lambda h: h.tensor_tensor(
            gst, psum[:, 3072:4096], h1s, ALU.mult),
            reads=[PB[6], PB[7], H1S], writes=[GSTB, PB[6], PB[7]])
        em.emit("scalar", lambda h, j=j: h.copy(
            hb_h[:, 1024 * j:1024 * (j + 1)], gst),
            reads=[GSTB], writes=[HB[j]])
        em.emit("vector", lambda h, j=j: h.tensor_tensor(
            hb_l[:, 1024 * j:1024 * (j + 1)], gst,
            hb_h[:, 1024 * j:1024 * (j + 1)], ALU.subtract),
            reads=[GSTB, HB[j]], writes=[HBL[j]])

    def out_stage(tis, gcol_fn):
        for k, ti in enumerate(tis):
            for dh in range(2):
                b = (2 * k + dh) % 4
                nmm = 0
                for j in range(2):
                    lh = hb_h[:, 1024 * j + 128 * k:1024 * j + 128 * (k + 1)]
                    ll = hb_l[:, 1024 * j + 128 * k:1024 * j + 128 * (k + 1)]
                    rh_ = w3h[:, 1024 * j + 512 * dh:1024 * j + 512 * (dh + 1)]
                    rl_ = w3l[:, 1024 * j + 512 * dh:1024 * j + 512 * (dh + 1)]
                    for lw, rw, rb in ((lh, rh_, [HB[j], W3C[j]]),
                                       (lh, rl_, [HB[j], W3L[j]]),
                                       (ll, rh_, [HBL[j], W3C[j]])):
                        nmm += 1
                        em.emit("tensor",
                                lambda h, lw=lw, rw=rw, b=b, s=(nmm == 1),
                                e=(nmm == 6):
                                h.matmul(bank(b), lw, rw, start=s, stop=e),
                                reads=rb, writes=[PB[b]])
                gcol, grd = gcol_fn(ti)
                xs = x_sb[:, 1024 * ti + 512 * dh:1024 * ti + 512 * (dh + 1)]
                em.emit("vector", lambda h, b=b, gcol=gcol, xs=xs:
                        h.scalar_tensor_tensor(xs, bank(b), gcol, xs,
                                               ALU.mult, ALU.add),
                        reads=[PB[b], XB[ti]] + grd, writes=[XB[ti], PB[b]])

    def ffn_call():
        for ti in range(NT):
            ln_tile(ti)
            router_tile(ti)
        w1v = dw1T.rearrange("(dc p) hh -> p dc hh", p=128)
        w2v = dw2T.rearrange("(dc p) hh -> p dc hh", p=128)
        for half in range(2):
            tis = list(range(8 * half, 8 * half + 8))
            for g in range(16):
                for j in range(2):
                    hc = 2 * g + j
                    c1h = wsh[:, 1024 * j:1024 * (j + 1)]
                    c1l = wsl[:, 1024 * j:1024 * (j + 1)]
                    c2h = wsh[:, 2048 + 1024 * j:3072 + 1024 * j]
                    c2l = wsl[:, 2048 + 1024 * j:3072 + 1024 * j]
                    wload_split(
                        w1v[:, :, 128 * hc:128 * (hc + 1)], c1h, c1l,
                        WS4[j], WS4L[j])
                    wload_split(
                        w2v[:, :, 128 * hc:128 * (hc + 1)], c2h, c2l,
                        WS4[2 + j], WS4L[2 + j])
                    h_stage(j, (WS4[j], WS4[2 + j]), (WS4L[j], WS4L[2 + j]),
                            c1h, c1l, c2h, c2l, tis)
                    wload_split(dw3sT[128 * hc:128 * (hc + 1), :],
                                w3h[:, 1024 * j:1024 * (j + 1)],
                                w3l[:, 1024 * j:1024 * (j + 1)],
                                W3C[j], W3L[j])
                out_stage(tis, lambda ti: (0.5, []))
            for e in range(NE):
                wv = dw12T[e].rearrange("(dc p) hh -> p dc hh", p=128)
                for g2 in range(4):
                    for j in range(2):
                        rhc = 2 * g2 + j
                        c1h = wsh[:, 2048 * j:2048 * j + 1024]
                        c1l = wsl[:, 2048 * j:2048 * j + 1024]
                        c2h = wsh[:, 2048 * j + 1024:2048 * (j + 1)]
                        c2l = wsl[:, 2048 * j + 1024:2048 * (j + 1)]
                        wload_split(
                            wv[:, :, 128 * rhc:128 * (rhc + 1)],
                            c1h, c1l, WS4[2 * j], WS4L[2 * j])
                        wload_split(
                            wv[:, :, RH + 128 * rhc:RH + 128 * (rhc + 1)],
                            c2h, c2l, WS4[2 * j + 1], WS4L[2 * j + 1])
                        h_stage(j, (WS4[2 * j], WS4[2 * j + 1]),
                                (WS4L[2 * j], WS4L[2 * j + 1]),
                                c1h, c1l, c2h, c2l, tis)
                        wload_split(dw3eT[e][128 * rhc:128 * (rhc + 1), :],
                                    w3h[:, 1024 * j:1024 * (j + 1)],
                                    w3l[:, 1024 * j:1024 * (j + 1)],
                                    W3C[j], W3L[j])
                    out_stage(tis, lambda ti, e=e: (
                        gates05[:, NE * ti + e:NE * ti + e + 1], [GB[ti]]))

    def attn_block():
        for ti in range(NT):
            ln_tile(ti)
        slopes = [2.0 ** (-(hh + 1) / 2.0) for hh in range(HEADS)]
        for p in range(8):
            wslices = []
            for qi_, dw in enumerate((dwqT, dwkT, dwvT)):
                wvw = dw.rearrange("(dc p2) o -> p2 dc o", p2=128)
                ch = wsh[:, 1024 * qi_:1024 * (qi_ + 1)]
                cl = wsl[:, 1024 * qi_:1024 * (qi_ + 1)]
                wload_split(
                    wvw[:, :, 128 * p:128 * (p + 1)], ch, cl,
                    WS4[qi_], WS4L[qi_])
                wslices.append((ch, cl))
            woh = wsh[:, 3072:4096]
            wol = wsl[:, 3072:4096]
            wload_split(dwoT[128 * p:128 * (p + 1), :], woh, wol,
                        WS4[3], WS4L[3])
            (wqh, wql), (wkh, wkl), (wvh, wvl) = wslices
            for ts4 in range(4):
                rdh = [AB[t] for t in range(4 * ts4, 4 * ts4 + 4)]
                rdl = [ABL[t] for t in range(4 * ts4, 4 * ts4 + 4)]
                for wi, (dsth, dstl, hbl, lbl, boff) in enumerate(
                        ((qth, qtl, [HB[0], HB[1]], [HBL[0], HBL[1]], 4),
                         (kth, ktl, [W3C[0], W3C[1]], [W3L[0], W3L[1]], 6))):
                    wh_ = wsh[:, 1024 * wi:1024 * (wi + 1)]
                    wl_ = wsl[:, 1024 * wi:1024 * (wi + 1)]
                    b = boff + ts4 % 2
                    for dc in range(8):
                        ah = actT_h[:, dc * TW + 512 * ts4:
                                    dc * TW + 512 * (ts4 + 1)]
                        al = actT_l[:, dc * TW + 512 * ts4:
                                    dc * TW + 512 * (ts4 + 1)]
                        for pi, (lw, rw, rb) in enumerate(
                                ((wh_, ah, [WS4[wi]] + rdh),
                                 (wh_, al, [WS4[wi]] + rdl),
                                 (wl_, ah, [WS4L[wi]] + rdh))):
                            em.emit("tensor",
                                    lambda h, dc=dc, b=b, lw=lw, rw=rw, pi=pi:
                                    h.matmul(bank(b),
                                             lw[:, 128 * dc:128 * (dc + 1)],
                                             rw,
                                             start=(dc == 0 and pi == 0),
                                             stop=(dc == 7 and pi == 2)),
                                    reads=rb, writes=[PB[b]])
                    em.emit("scalar", lambda h, dsth=dsth, ts4=ts4, b=b:
                            h.copy(dsth[:, 512 * ts4:512 * (ts4 + 1)],
                                   bank(b)),
                            reads=[PB[b]], writes=list(hbl))
                    em.emit("vector",
                            lambda h, dsth=dsth, dstl=dstl, ts4=ts4, b=b:
                            h.tensor_tensor(dstl[:, 512 * ts4:512 * (ts4 + 1)],
                                            bank(b),
                                            dsth[:, 512 * ts4:512 * (ts4 + 1)],
                                            ALU.subtract),
                            reads=[PB[b]] + list(hbl),
                            writes=list(lbl) + [PB[b]])
            for k in range(NT):
                b = k % 2
                for dc in range(8):
                    ah = athv[:, dc, 128 * k:128 * (k + 1)]
                    al = atlv[:, dc, 128 * k:128 * (k + 1)]
                    vh = wvh[:, 128 * dc:128 * (dc + 1)]
                    vl = wvl[:, 128 * dc:128 * (dc + 1)]
                    for pi, (lw, rw, rb) in enumerate(
                            ((ah, vh, [WS4[2], AB[k]]),
                             (ah, vl, [WS4L[2], AB[k]]),
                             (al, vh, [WS4[2], ABL[k]]))):
                        em.emit("tensor",
                                lambda h, b=b, lw=lw, rw=rw, dc=dc, pi=pi:
                                h.matmul(bank(b, 128), lw, rw,
                                         start=(dc == 0 and pi == 0),
                                         stop=(dc == 7 and pi == 2)),
                                reads=rb, writes=[PB[b]])
                em.emit("scalar", lambda h, k=k, b=b: h.copy(
                    v_sb[:, 128 * k:128 * (k + 1)], bank(b, 128)),
                    reads=[PB[b]], writes=[VBL[k // 8], PB[b]])
            def emit_scores(qi, hi, par):
                hof = 64 * hi
                slope = slopes[2 * p + hi]
                sr = srow if par == 0 else srow_b
                kmax = 128 * (qi + 1)
                nkj = ceil_div(kmax, 512)
                em.emit("vector", lambda h, slope=slope: h.tensor_scalar(
                    rs512, iotaD, slope, None, ALU.mult, ALU.bypass),
                    reads=[CONST], writes=[RSB])
                for kj in range(nkj):
                    w = min(512, kmax - 512 * kj)
                    b = 4 + kj
                    qh_ = qth[hof:hof + 64, 128 * qi:128 * (qi + 1)]
                    ql_ = qtl[hof:hof + 64, 128 * qi:128 * (qi + 1)]
                    kh_ = kth[hof:hof + 64, 512 * kj:512 * kj + w]
                    kl_ = ktl[hof:hof + 64, 512 * kj:512 * kj + w]
                    for pi, (lw, rw) in enumerate(
                            ((qh_, kh_), (qh_, kl_), (ql_, kh_))):
                        em.emit("tensor",
                                lambda h, w=w, b=b, lw=lw, rw=rw, pi=pi:
                                h.matmul(bank(b, w), lw, rw,
                                         start=(pi == 0), stop=(pi == 2)),
                                reads=QTBL + KTBL, writes=[PB[b]])
                    em.emit("vector", lambda h, kj=kj, w=w, b=b, s=slope,
                            sr=sr:
                            h.tensor_scalar(
                                sr[:, 512 * kj:512 * kj + w],
                                bank(b, w), 0.125, s * 512.0 * kj,
                                ALU.mult, ALU.add),
                            reads=[PB[b]], writes=[SRB2[par], PB[b]])
                    em.emit("vector", lambda h, kj=kj, w=w, sr=sr:
                            h.tensor_tensor(
                                sr[:, 512 * kj:512 * kj + w],
                                sr[:, 512 * kj:512 * kj + w],
                                rs512[:, 0:w], ALU.add),
                            reads=[SRB2[par], RSB], writes=[SRB2[par]])
                    if kj == nkj - 1:
                        em.emit("vector", lambda h, kmax=kmax, sr=sr:
                                h.tensor_tensor(sr[:, kmax - 128:kmax],
                                                sr[:, kmax - 128:kmax],
                                                mask512, ALU.add),
                                reads=[SRB2[par], CONST], writes=[SRB2[par]])
                    em.emit("vector", lambda h, kj=kj, w=w, sr=sr, par=par:
                            h.reduce_max(mch[:, 4 * par + kj:4 * par + kj + 1],
                                         sr[:, 512 * kj:512 * kj + w],
                                         axis=AX.X),
                            reads=[SRB2[par]], writes=[STQ[par]])
                em.emit("vector", lambda h, par=par, nkj=nkj: h.reduce_max(
                    mrow, mch[:, 4 * par:4 * par + nkj], axis=AX.X),
                    reads=[STQ[par]], writes=[STQ[par]])
                em.emit("vector", lambda h, par=par: h.tensor_scalar(
                    negm2[:, par:par + 1], mrow, -1.0, None, ALU.mult,
                    ALU.bypass),
                    reads=[STQ[par]], writes=[STQ[par]])

            def emit_epilogue(qi, hi, par):
                hof = 64 * hi
                sr = srow if par == 0 else srow_b
                kmax = 128 * (qi + 1)
                nkj = ceil_div(kmax, 512)
                nkt = kmax // 128
                for kj in range(nkj):
                    w = min(512, kmax - 512 * kj)
                    em.emit("scalar", lambda h, kj=kj, w=w, sr=sr, par=par:
                            h.activation(out=sr[:, 512 * kj:512 * kj + w],
                                         in_=sr[:, 512 * kj:512 * kj + w],
                                         func=AF.Exp,
                                         bias=negm2[:, par:par + 1], scale=1.0,
                                         accum_out=sch[:, kj:kj + 1]),
                            reads=[SRB2[par], STQ[par]],
                            writes=[SRB2[par], SCHB])
                em.emit("vector", lambda h, nkj=nkj: h.tensor_reduce(
                    sumr, sch[:, 0:nkj], axis=AX.X, op=ALU.add),
                    reads=[SCHB], writes=[SCHB])
                em.emit("vector", lambda h, hi=hi: h.reciprocal(
                    recip[:, hi:hi + 1], sumr),
                    reads=[SCHB], writes=[RECB])
                for kt4 in range(ceil_div(nkt, 4)):
                    b = kt4 % 2
                    n = min(4, nkt - 4 * kt4)
                    sl = PT2[kt4 % 2]
                    for j in range(n):
                        kt = 4 * kt4 + j
                        em.emit("tensor", lambda h, kt=kt, j=j, b=b, sr=sr:
                                h.transpose(
                                    bank(b)[:, 128 * j:128 * (j + 1)],
                                    sr[:, 128 * kt:128 * (kt + 1)],
                                    ident),
                                reads=[SRB2[par], CONST], writes=[PB[b]])
                    em.emit("vector", lambda h, kt4=kt4, b=b, n=n:
                            h.tensor_copy(
                                ptb[:, 512 * (kt4 % 2):
                                    512 * (kt4 % 2) + 128 * n],
                                bank(b, 128 * n)),
                            reads=[PB[b]], writes=[sl, PB[b]])
                    for j in range(n):
                        kt = 4 * kt4 + j
                        em.emit("tensor",
                                lambda h, kt=kt, j=j, kt4=kt4, hof=hof,
                                hi=hi, nkt=nkt:
                                h.matmul(
                                    bank(2, 128)[:, 64 * hi:64 * (hi + 1)],
                                    ptb[:, 512 * (kt4 % 2) + 128 * j:
                                        512 * (kt4 % 2) + 128 * (j + 1)],
                                    v_sb[:, 128 * kt + hof:
                                         128 * kt + hof + 64],
                                    start=(kt == 0), stop=(kt == nkt - 1)),
                                reads=[sl, VBL[kt // 8]], writes=[PB[2]])
                em.emit("vector", lambda h, hi=hi: h.tensor_scalar(
                    ocomb[:, 64 * hi:64 * (hi + 1)],
                    bank(2, 128)[:, 64 * hi:64 * (hi + 1)],
                    recip[:, hi:hi + 1], None, ALU.mult, ALU.bypass),
                    reads=[PB[2], RECB], writes=[OCB, PB[2]])
                if hi == 1:
                    em.emit("tensor", lambda h: h.transpose(bank(3, 128),
                                                            ocomb, ident),
                            reads=[OCB, CONST], writes=[PB[3]])
                    em.emit("scalar", lambda h: h.copy(otc_h, bank(3, 128)),
                            reads=[PB[3]], writes=[OTB])
                    em.emit("vector", lambda h: h.tensor_tensor(
                        otc_l, bank(3, 128), otc_h, ALU.subtract),
                        reads=[PB[3], OTB], writes=[OTBL, PB[3]])
                    for dh in range(2):
                        rh_ = woh[:, 512 * dh:512 * (dh + 1)]
                        rl_ = wol[:, 512 * dh:512 * (dh + 1)]
                        for pi, (lw, rw, rb) in enumerate(
                                ((otc_h, rh_, [OTB, WS4[3]]),
                                 (otc_h, rl_, [OTB, WS4L[3]]),
                                 (otc_l, rh_, [OTBL, WS4[3]]))):
                            em.emit("tensor",
                                    lambda h, lw=lw, rw=rw, pi=pi:
                                    h.matmul(bank(3), lw, rw,
                                             start=(pi == 0), stop=(pi == 2)),
                                    reads=rb, writes=[PB[3]])
                        xs = x_sb[:, 1024 * qi + 512 * dh:
                                  1024 * qi + 512 * (dh + 1)]
                        em.emit("vector", lambda h, xs=xs:
                                h.scalar_tensor_tensor(xs, bank(3), 1.0, xs,
                                                       ALU.mult, ALU.add),
                                reads=[PB[3], XB[qi]], writes=[XB[qi], PB[3]])

            its = [(qi, hi) for qi in range(NT) for hi in range(2)]
            prev = None
            for idx, (qi, hi) in enumerate(its):
                emit_scores(qi, hi, idx % 2)
                if prev is not None:
                    emit_epilogue(prev[0], prev[1], (idx - 1) % 2)
                prev = (qi, hi)
            emit_epilogue(prev[0], prev[1], (len(its) - 1) % 2)
    ffn_call()
    attn_block()
    ffn_call()
    ov = dout.rearrange("(t p) d -> p t d", p=128)
    for ti in range(NT):
        em.dma("gpsimd", ov[:, ti, :], x_sb[:, 1024 * ti:1024 * (ti + 1)],
               reads=[XB[ti]])
    final_q = em.dma_cnt["qG"]

    stack = contextlib.ExitStack()
    sems = {}
    for name in ["s_tensor", "s_vector", "s_scalar", "s_gpsimd", "s_sync",
                 "qS", "qG", "qA"]:
        sems[name] = stack.enter_context(nc.semaphore(name))
    block = stack.enter_context(nc.Block())
    fin = Op("gpsimd", lambda h: h.nop(), len(em.ops["gpsimd"]))
    fin.waits_dma["qG"] = final_q
    em.ops["gpsimd"].append(fin)
    em.finalize_and_replay(block, sems)
    stack.close()

    nc.m.detect_race_conditions = False
    return nc


_CACHE = {}


def _get_nc():
    if "full" not in _CACHE:
        _CACHE["full"] = build_program()
    return _CACHE["full"]


def kernel(L0_attn_x, L0_attn_W_Q, L0_attn_W_K, L0_attn_W_V, L0_attn_W_O,
           L0_ffn_w12, L0_ffn_w3, L0_ffn_shared_w1, L0_ffn_shared_w2,
           L0_ffn_shared_w3, L0_ffn_router_w, L0_ffn_expert_bias):
    x = np.ascontiguousarray(np.asarray(L0_attn_x, dtype=np.float32))

    def T(a):
        return np.ascontiguousarray(np.asarray(a, dtype=np.float32).T)

    w = {
        "wqT": T(L0_attn_W_Q), "wkT": T(L0_attn_W_K), "wvT": T(L0_attn_W_V),
        "woT": T(L0_attn_W_O),
        "w1T": T(L0_ffn_shared_w1), "w2T": T(L0_ffn_shared_w2),
        "w3sT": T(L0_ffn_shared_w3),
        "w12T": np.ascontiguousarray(
            np.asarray(L0_ffn_w12, dtype=np.float32).transpose(0, 2, 1)),
        "w3eT": np.ascontiguousarray(
            np.asarray(L0_ffn_w3, dtype=np.float32).transpose(0, 2, 1)),
        "routerT": T(L0_ffn_router_w),
        "ebias": np.ascontiguousarray(
            np.asarray(L0_ffn_expert_bias, dtype=np.float32).reshape(1, NE)),
    }
    nc = _get_nc()
    in_maps = [dict(w, x_in=np.ascontiguousarray(x[b])) for b in range(NCORES)]
    res = run_bass_kernel_spmd(nc, in_maps, core_ids=list(range(NCORES)))
    out = np.stack([res.results[b]["x_out"] for b in range(NCORES)], axis=0)
    return out.astype(np.float32)


# version 06
# speedup vs baseline: 1.5382x; 1.2811x over previous
import os
os.environ.setdefault("BASS_DISABLE_FRAME_TO_TRACEBACK", "1")

import contextlib
import numpy as np

import concourse.bass as bass
import concourse.mybir as mybir
from concourse.bass import IndirectOffsetOnAxis
from concourse.bass_utils import run_bass_kernel_spmd

F32 = mybir.dt.float32
BF16 = mybir.dt.bfloat16
I32 = mybir.dt.int32
AF = mybir.ActivationFunctionType
ALU = mybir.AluOpType
AX = mybir.AxisListType

NCORES = 8
S = 2048
D = 1024
NT = S // 128
HEADS = 16
HID = 4096
RH = 1024
NE = 8
EPS = 1e-5
NEG = -1.0e30
TW = 2048

ENGINES = ("tensor", "vector", "scalar", "gpsimd", "sync")
DMA_QUEUES = {"sync": "qS", "gpsimd": "qG", "scalar": "qA"}


class Op:
    __slots__ = ("eng", "fn", "drain", "waits_dma", "waits_op", "inc_needed",
                 "sem_val", "seq", "qsem", "qval")

    def __init__(self, eng, fn, seq):
        self.eng = eng
        self.fn = fn
        self.drain = False
        self.waits_dma = {}
        self.waits_op = []
        self.inc_needed = False
        self.sem_val = None
        self.seq = seq
        self.qsem = None
        self.qval = None


class Buf:
    __slots__ = ("w", "r")

    def __init__(self):
        self.w = None
        self.r = []


class Emitter:
    def __init__(self):
        self.ops = {e: [] for e in ENGINES}
        self.dma_cnt = {q: 0 for q in DMA_QUEUES.values()}
        self.drain_cover = {e: -1 for e in ENGINES}

    def _deps(self, op, reads, writes):
        toks = []
        for b in reads:
            if b.w is not None:
                toks.append(b.w)
        for b in writes:
            if b.w is not None:
                toks.append(b.w)
            toks.extend(b.r)
        for t in toks:
            if t[0] == "dma":
                _, q, v = t
                op.waits_dma[q] = max(op.waits_dma.get(q, 0), v)
            else:
                p = t[1]
                if p.eng == op.eng:
                    if op.eng != "tensor" and p.seq > self.drain_cover[op.eng]:
                        op.drain = True
                else:
                    p.inc_needed = True
                    op.waits_op.append(p)
        if op.drain:
            self.drain_cover[op.eng] = op.seq - 1

    def emit(self, eng, fn, reads=(), writes=()):
        op = Op(eng, fn, len(self.ops[eng]))
        self._deps(op, reads, writes)
        self.ops[eng].append(op)
        tok = ("op", op)
        for b in writes:
            b.w = tok
            b.r = []
        for b in reads:
            b.r.append(tok)
        return op

    def dma(self, eng, out_ap, in_ap, reads=(), writes=()):
        q = DMA_QUEUES[eng]
        op = Op(eng, None, len(self.ops[eng]))
        self._deps(op, reads, writes)
        self.dma_cnt[q] += 16
        op.qsem = q
        op.qval = self.dma_cnt[q]

        def fn(h, out_ap=out_ap, in_ap=in_ap):
            return h.dma_start(out=out_ap, in_=in_ap)
        op.fn = fn
        self.ops[eng].append(op)
        tok = ("dma", q, op.qval)
        for b in writes:
            b.w = tok
            b.r = []
        for b in reads:
            b.r.append(tok)
        return op

    def dma_op(self, eng, fn, reads=(), writes=()):
        q = DMA_QUEUES[eng]
        op = Op(eng, fn, len(self.ops[eng]))
        self._deps(op, reads, writes)
        self.dma_cnt[q] += 16
        op.qsem = q
        op.qval = self.dma_cnt[q]
        self.ops[eng].append(op)
        tok = ("dma", q, op.qval)
        for b in writes:
            b.w = tok
            b.r = []
        for b in reads:
            b.r.append(tok)
        return op

    def finalize_and_replay(self, block, sems):
        for e in ENGINES:
            v = 0
            for op in self.ops[e]:
                if op.inc_needed:
                    v += 1
                    op.sem_val = v
        esem = {e: sems["s_" + e] for e in ENGINES}
        qsems = {q: sems[q] for q in self.dma_cnt}

        def body(e):
            ops = self.ops[e]

            def run(h):
                watermark = {}
                for op in ops:
                    targets = {}
                    for q, v in op.waits_dma.items():
                        if watermark.get(q, 0) < v:
                            targets[qsems[q]] = max(targets.get(qsems[q], 0), v)
                            watermark[q] = v
                    for p in op.waits_op:
                        key = "s_" + p.eng
                        if watermark.get(key, 0) < p.sem_val:
                            targets[esem[p.eng]] = max(
                                targets.get(esem[p.eng], 0), p.sem_val)
                            watermark[key] = p.sem_val
                    for s, v in targets.items():
                        h.wait_ge(s, v)
                    if op.drain:
                        h.drain()
                    instr = op.fn(h)
                    if op.qsem is not None:
                        instr.then_inc(qsems[op.qsem], 16)
                    elif op.inc_needed:
                        instr.then_inc(esem[op.eng], 1)
            return run

        for e in ENGINES:
            getattr(block, e)(body(e))


def ceil_div(a, b):
    return (a + b - 1) // b


def build_program():
    nc = bass.Bass("TRN2", target_bir_lowering=False, debug=False)

    dx = nc.dram_tensor("x_in", [S, D], F32, kind="ExternalInput").ap()
    dout = nc.dram_tensor("x_out", [S, D], F32, kind="ExternalOutput").ap()
    dwqT = nc.dram_tensor("wqT", [D, D], F32, kind="ExternalInput").ap()
    dwkT = nc.dram_tensor("wkT", [D, D], F32, kind="ExternalInput").ap()
    dwvT = nc.dram_tensor("wvT", [D, D], F32, kind="ExternalInput").ap()
    dwoT = nc.dram_tensor("woT", [D, D], F32, kind="ExternalInput").ap()
    dw1T = nc.dram_tensor("w1T", [D, HID], F32, kind="ExternalInput").ap()
    dw2T = nc.dram_tensor("w2T", [D, HID], F32, kind="ExternalInput").ap()
    dw3sT = nc.dram_tensor("w3sT", [HID, D], F32, kind="ExternalInput").ap()
    dw12T = nc.dram_tensor("w12T", [NE, D, 2 * RH], F32, kind="ExternalInput").ap()
    dw3eT = nc.dram_tensor("w3eT", [NE, RH, D], F32, kind="ExternalInput").ap()
    drt = nc.dram_tensor("routerT", [D, NE], F32, kind="ExternalInput").ap()
    debias = nc.dram_tensor("ebias", [1, NE], F32, kind="ExternalInput").ap()
    xe_hbm = nc.dram_tensor("xe_scr", [NE * 1024, D], F32, kind="Internal").ap()
    xg_hbm = nc.dram_tensor("xg_scr", [NE * 1024, NE], F32, kind="Internal").ap()
    xo_hbm = nc.dram_tensor("xo_scr", [NE * 1024, D], F32, kind="Internal").ap()

    A = nc.alloc_sbuf_tensor
    x_sb = A("x_sb", [128, NT * 1024], F32).ap()
    actT_h = A("actT_h", [128, 8 * TW], BF16).ap()
    actT_l = A("actT_l", [128, 8 * TW], BF16).ap()
    hb_h = A("hb_h", [128, 2 * 1024], BF16).ap()
    hb_l = A("hb_l", [128, 2 * 1024], BF16).ap()
    w3h = A("w3h", [128, 2 * 1024], BF16).ap()
    w3l = A("w3l", [128, 2 * 1024], BF16).ap()
    wsh = A("wsh", [128, 4 * 1024], BF16).ap()
    wsl = A("wsl", [128, 4 * 1024], BF16).ap()
    wf = A("wf", [128, 2048], F32).ap()
    h1s = A("h1s", [128, 1024], F32).ap()
    gst = A("gst", [128, 1024], F32).ap()
    srow = A("srow", [128, S], F32).ap()
    ptb = A("ptb", [128, 2 * 512], F32).ap()
    srow_b = A("srow_b", [128, S], F32).ap()
    otc_h = A("otc_h", [128, 128], BF16).ap()
    otc_l = A("otc_l", [128, 128], BF16).ap()
    mch = A("mch", [128, 8], F32).ap()
    sch = A("sch", [128, 4], F32).ap()
    rs512 = A("rs512", [128, 512], F32).ap()
    mask512 = A("mask512", [128, 128], F32).ap()
    v_sb = wf
    ident = A("ident", [128, 128], F32).ap()
    iotaD = A("iotaD", [128, 512], I32).ap()
    ones1 = A("ones1", [1, 128], F32).ap()
    ocomb = A("ocomb", [128, 128], F32).ap()
    rt_sb = A("rt_sb", [128, 8 * NE], F32).ap()
    biasB = A("biasB", [128, NE], F32).ap()
    bias1 = A("bias1", [1, NE], F32).ap()
    gates05 = A("gates05", [128, NT * NE], F32).ap()
    stats6 = A("stats6", [128, 12], F32).ap()
    stats2 = A("stats2", [128, 2], F32).ap()
    sqv = A("sqv", [128, 1], F32).ap()
    rstd = A("rstd", [128, 1], F32).ap()
    sc_sb = A("sc_sb", [128, NE], F32).ap()
    selb = A("selb", [128, NE], F32).ap()
    tmask = A("tmask", [128, NE], F32).ap()
    masked = A("masked", [128, NE], F32).ap()
    maskf = A("maskf", [128, NE], F32).ap()
    m1c = A("m1c", [128, 1], F32).ap()
    m2c = A("m2c", [128, 1], F32).ap()
    cntc = A("cntc", [128, 1], F32).ap()
    selc = A("selc", [128, 1], F32).ap()
    selc2 = A("selc2", [128, 1], F32).ap()
    tri128 = A("tri128", [128, 128], F32).ap()
    ones128 = A("ones128", [128, 128], F32).ap()
    ptok = A("ptok", [128, NT * NE], F32).ap()
    mk_sb = A("mk_sb", [128, NT * NE], F32).ap()
    pfx_sb = A("pfx_sb", [128, 128], F32).ap()
    iota8i = A("iota8i", [128, NE], I32).ap()
    iota8 = A("iota8", [128, NE], F32).ap()
    offs = A("offs", [128, 2 * NT], mybir.dt.uint32).ap()
    offf = A("offf", [128, 1], F32).ap()
    gsb = A("gsb", [128, 6 * NE], F32).ap()
    e0c = A("e0c", [128, 1], F32).ap()
    e1c = A("e1c", [128, 1], F32).ap()
    s0c = A("s0c", [128, 1], F32).ap()
    c0c = A("c0c", [128, 1], F32).ap()
    mrow = A("mrow", [128, 1], F32).ap()
    negm = A("negm", [128, 1], F32).ap()
    negm2 = A("negm2", [128, 2], F32).ap()
    sumr = A("sumr", [128, 1], F32).ap()
    recip = A("recip", [128, 2], F32).ap()

    qth, qtl = hb_h, hb_l
    kth, ktl = w3h, w3l

    psum = nc.alloc_psum_tensor("psum", [128, 4096], F32).ap()

    def bank(i, w=512):
        return psum[:, 512 * i:512 * i + w]

    em = Emitter()

    XB = [Buf() for _ in range(NT)]
    AB = [Buf() for _ in range(NT)]
    ABL = [Buf() for _ in range(NT)]
    PB = [Buf() for _ in range(8)]
    HB = [Buf() for _ in range(2)]
    HBL = [Buf() for _ in range(2)]
    W3C = [Buf() for _ in range(2)]
    W3L = [Buf() for _ in range(2)]
    WS4 = [Buf() for _ in range(4)]
    WS4L = [Buf() for _ in range(4)]
    WFB = [Buf(), Buf()]
    GB = [Buf() for _ in range(NT)]
    PT2 = [Buf() for _ in range(2)]
    STATS = Buf(); SC = Buf(); TOPB = Buf()
    SRB = Buf()
    OTB = Buf(); OTBL = Buf(); OCB = Buf(); RSB = Buf(); H1S = Buf()
    GSTB = Buf()
    SRH = [[Buf(), Buf()], [Buf(), Buf()]]

    STQ = [Buf(), Buf()]
    SCHB = Buf()
    CONST = Buf(); RTB = Buf(); BB = Buf(); RECB = Buf()
    MKB = [Buf() for _ in range(NT)]
    ACTXTRA = []
    PTOKB = Buf(); PFXB = Buf(); OFFB = Buf(); RTT = Buf()
    XEB = Buf(); XGB = Buf(); XOB = Buf(); GSBB = Buf()
    QTBL = [HB[0], HB[1], HBL[0], HBL[1]]
    KTBL = [W3C[0], W3C[1], W3L[0], W3L[1]]
    VBL = WFB

    em.emit("gpsimd", lambda h: h.iota(iotaD, pattern=[[1, 512]], base=0,
                                       channel_multiplier=-1), writes=[CONST])
    em.emit("vector", lambda h: h.tensor_scalar(
        ident, iotaD[:, 0:128], 0, None, ALU.is_equal, ALU.bypass),
        reads=[CONST], writes=[CONST])
    rt_view = drt.rearrange("(dc p) e -> p dc e", p=128)
    em.dma("sync", rt_sb.rearrange("p (dc e) -> p dc e", e=NE), rt_view,
           writes=[RTB])
    em.dma("sync", bias1, debias, writes=[BB])
    em.emit("vector", lambda h: h.memset(ones1, 1.0), writes=[CONST])
    em.emit("tensor", lambda h: h.matmul(bank(0, NE), ones1, bias1,
                                         start=True, stop=True),
            reads=[BB, CONST], writes=[PB[0]])
    em.emit("vector", lambda h: h.tensor_copy(biasB, bank(0, NE)),
            reads=[PB[0]], writes=[BB, PB[0]])
    em.emit("vector", lambda h: h.memset(ocomb, 0.0), writes=[OCB])
    em.emit("vector", lambda h: h.tensor_scalar(
        mask512, iotaD[:, 0:128], 0, NEG, ALU.is_gt, ALU.mult),
        reads=[CONST], writes=[CONST])
    em.emit("vector", lambda h: h.tensor_scalar(
        tri128, iotaD[:, 0:128], 0, None, ALU.is_ge, ALU.bypass),
        reads=[CONST], writes=[CONST])
    em.emit("vector", lambda h: h.memset(ones128, 1.0), writes=[CONST])
    em.emit("gpsimd", lambda h: h.iota(iota8i, pattern=[[1, NE]], base=0,
                                       channel_multiplier=0), writes=[RTT])
    em.emit("vector", lambda h: h.tensor_copy(iota8, iota8i),
            reads=[RTT], writes=[CONST])
    xv = dx.rearrange("(t p) d -> p t d", p=128)
    for ti in range(NT):
        em.dma("gpsimd", x_sb[:, 1024 * ti:1024 * (ti + 1)], xv[:, ti, :],
               writes=[XB[ti]])

    athv = actT_h.rearrange("p (dc t) -> p dc t", t=TW)
    atlv = actT_l.rearrange("p (dc t) -> p dc t", t=TW)

    def ln_tile(ti):
        tcol = 128 * ti
        xt = x_sb[:, 1024 * ti:1024 * (ti + 1)]
        em.emit("vector", lambda h: h.bn_stats(stats6[:, 0:6], xt[:, 0:512]),
                reads=[XB[ti]], writes=[STATS])
        em.emit("vector", lambda h: h.bn_stats(stats6[:, 6:12], xt[:, 512:1024]),
                reads=[XB[ti]], writes=[STATS])
        em.emit("vector", lambda h: h.bn_aggr(stats2, stats6),
                reads=[STATS], writes=[STATS])
        em.emit("vector", lambda h: h.tensor_scalar(
            sqv, stats2[:, 1:2], EPS, None, ALU.add, ALU.bypass),
            reads=[STATS], writes=[STATS])
        em.emit("scalar", lambda h: h.activation(out=sqv, in_=sqv,
                                                 func=AF.Sqrt),
                reads=[STATS], writes=[STATS])
        em.emit("vector", lambda h: h.reciprocal(rstd, sqv),
                reads=[STATS], writes=[STATS])
        em.emit("vector", lambda h: h.tensor_scalar(
            h1s, xt, stats2[:, 0:1], rstd, ALU.subtract, ALU.mult),
            reads=[XB[ti], STATS], writes=[H1S])
        for half in range(2):
            b = 6 + half
            for j in range(4):
                dc = 4 * half + j
                em.emit("tensor", lambda h, dc=dc, j=j, b=b: h.transpose(
                    bank(b)[:, 128 * j:128 * (j + 1)],
                    h1s[:, 128 * dc:128 * (dc + 1)], ident),
                    reads=[H1S, CONST], writes=[PB[b]])
            em.emit("vector", lambda h, half=half, b=b: h.tensor_copy(
                gst[:, 512 * half:512 * (half + 1)], bank(b)),
                reads=[PB[b]], writes=[GSTB, PB[b]])
        gv = gst.rearrange("p (dc t) -> p dc t", t=128)
        em.emit("scalar", lambda h, tcol=tcol: h.copy(
            athv[:, 0:8, tcol:tcol + 128], gv),
            reads=[GSTB], writes=[AB[ti]] + ACTXTRA)
        em.emit("vector", lambda h, tcol=tcol: h.tensor_tensor(
            atlv[:, 0:8, tcol:tcol + 128], gv,
            athv[:, 0:8, tcol:tcol + 128], ALU.subtract),
            reads=[GSTB, AB[ti]], writes=[ABL[ti]] + ACTXTRA)

    def router_tile(ti):
        for dc in range(8):
            em.emit("tensor", lambda h, dc=dc: h.matmul(
                bank(5, NE), gst[:, 128 * dc:128 * (dc + 1)],
                rt_sb[:, dc * NE:(dc + 1) * NE],
                start=(dc == 0), stop=(dc == 7)),
                reads=[GSTB, RTB], writes=[PB[5]])
        em.emit("scalar", lambda h: h.activation(out=sc_sb, in_=bank(5, NE),
                                                 func=AF.Sigmoid),
                reads=[PB[5]], writes=[SC, PB[5]])
        em.emit("vector", lambda h: h.tensor_tensor(selb, sc_sb, biasB, ALU.add),
                reads=[SC, BB], writes=[TOPB])
        em.emit("vector", lambda h: h.reduce_max(m1c, selb, axis=AX.X),
                reads=[TOPB], writes=[TOPB])
        em.emit("vector", lambda h: h.tensor_scalar(
            tmask, selb, m1c, None, ALU.is_ge, ALU.bypass),
            reads=[TOPB], writes=[TOPB])
        em.emit("vector", lambda h: h.scalar_tensor_tensor(
            masked, tmask, NEG, selb, ALU.mult, ALU.add),
            reads=[TOPB], writes=[TOPB])
        em.emit("vector", lambda h: h.reduce_max(m2c, masked, axis=AX.X),
                reads=[TOPB], writes=[TOPB])
        em.emit("vector", lambda h: h.tensor_scalar(
            maskf, selb, m2c, None, ALU.is_ge, ALU.bypass),
            reads=[TOPB], writes=[TOPB])
        em.emit("vector", lambda h: h.tensor_reduce(
            cntc, tmask, axis=AX.X, op=ALU.add),
            reads=[TOPB], writes=[TOPB])
        em.emit("vector", lambda h: h.tensor_scalar(
            selc, cntc, 1.5, None, ALU.is_ge, ALU.bypass),
            reads=[TOPB], writes=[TOPB])
        em.emit("vector", lambda h: h.tensor_scalar(
            selc2, selc, -1.0, 1.0, ALU.mult, ALU.add),
            reads=[TOPB], writes=[TOPB])
        em.emit("vector", lambda h: h.tensor_scalar(
            maskf, maskf, selc2, None, ALU.mult, ALU.bypass),
            reads=[TOPB], writes=[TOPB])
        mk = mk_sb[:, NE * ti:NE * (ti + 1)]
        em.emit("vector", lambda h, mk=mk: h.scalar_tensor_tensor(
            mk, tmask, selc, maskf, ALU.mult, ALU.add),
            reads=[TOPB], writes=[MKB[ti]])
        g = gates05[:, NE * ti:NE * (ti + 1)]
        em.emit("vector", lambda h, g=g, mk=mk: h.scalar_tensor_tensor(
            g, sc_sb, 0.5, mk, ALU.mult, ALU.mult),
            reads=[SC, MKB[ti]], writes=[GB[ti]])

    def route_tile(ti):
        mk = mk_sb[:, NE * ti:NE * (ti + 1)]
        pt = ptok[:, NE * ti:NE * (ti + 1)]
        for sj in range(ti + 1):
            rhs = tri128 if sj == ti else ones128
            em.emit("tensor", lambda h, sj=sj, rhs=rhs, ti=ti: h.matmul(
                bank(5, 128)[0:8, :], mk_sb[:, NE * sj:NE * (sj + 1)], rhs,
                start=(sj == 0), stop=(sj == ti)),
                reads=[MKB[sj], CONST], writes=[PB[5]])
        em.emit("vector", lambda h: h.tensor_copy(pfx_sb[0:8, :],
                                                  bank(5, 128)[0:8, :]),
                reads=[PB[5]], writes=[PFXB, PB[5]])
        em.emit("tensor", lambda h: h.transpose(bank(5, NE), pfx_sb[0:8, :],
                                                ident[0:8, 0:8]),
                reads=[PFXB, CONST], writes=[PB[5]])
        em.emit("vector", lambda h, pt=pt: h.tensor_copy(pt, bank(5, NE)),
                reads=[PB[5]], writes=[PTOKB, PB[5]])
        for rank in range(2):
            if rank == 0:
                em.emit("vector", lambda h, mk=mk: h.tensor_scalar(
                    masked, mk, -1e6, 1e6, ALU.mult, ALU.add),
                    reads=[MKB[ti]], writes=[RTT])
                em.emit("vector", lambda h: h.tensor_tensor(
                    masked, masked, iota8, ALU.add),
                    reads=[RTT, CONST], writes=[RTT])
                em.emit("vector", lambda h: h.tensor_reduce(
                    e0c, masked, axis=AX.X, op=ALU.min),
                    reads=[RTT], writes=[RTT])
            else:
                em.emit("vector", lambda h, mk=mk: h.tensor_tensor(
                    tmask, iota8, mk, ALU.mult),
                    reads=[MKB[ti], CONST], writes=[RTT])
                em.emit("vector", lambda h, mk=mk: h.scalar_tensor_tensor(
                    masked, mk, 1e6, tmask, ALU.mult, ALU.add),
                    reads=[MKB[ti], RTT], writes=[RTT])
                em.emit("vector", lambda h: h.tensor_scalar(
                    masked, masked, 1.0, -1e6, ALU.mult, ALU.add),
                    reads=[RTT], writes=[RTT])
                em.emit("vector", lambda h: h.tensor_reduce(
                    e0c, masked, axis=AX.X, op=ALU.max),
                    reads=[RTT], writes=[RTT])
            em.emit("vector", lambda h: h.tensor_scalar(
                tmask, iota8, e0c, None, ALU.is_equal, ALU.bypass),
                reads=[RTT, CONST], writes=[RTT])
            em.emit("vector", lambda h, pt=pt: h.scalar_tensor_tensor(
                masked, tmask, 1.0, pt, ALU.mult, ALU.mult, accum_out=s0c),
                reads=[RTT, PTOKB], writes=[RTT])
            em.emit("vector", lambda h: h.tensor_scalar(
                c0c, s0c, -1.0, 1023.0, ALU.add, ALU.min),
                reads=[RTT], writes=[RTT])
            em.emit("vector", lambda h: h.scalar_tensor_tensor(
                offf, e0c, 1024.0, c0c, ALU.mult, ALU.add),
                reads=[RTT], writes=[RTT])
            oc = offs[:, 2 * ti + rank:2 * ti + rank + 1]
            em.emit("vector", lambda h, oc=oc: h.tensor_copy(oc, offf),
                    reads=[RTT], writes=[OFFB])
            em.dma_op("gpsimd", lambda h, oc=oc: h.indirect_dma_start(
                xe_hbm, IndirectOffsetOnAxis(oc, 0), h1s, None),
                reads=[H1S, OFFB], writes=[XEB])
            gt = gates05[:, NE * ti:NE * (ti + 1)]
            em.dma_op("gpsimd", lambda h, oc=oc, gt=gt: h.indirect_dma_start(
                xg_hbm, IndirectOffsetOnAxis(oc, 0), gt, None),
                reads=[GB[ti], OFFB], writes=[XGB])

    wpar = [0]

    def wload_split(dram_ap, wh_ap, wl_ap, wbuf, wbufl, extra=()):
        par = wpar[0]
        wpar[0] ^= 1
        wfs = wf[:, 1024 * par:1024 * (par + 1)]
        wfd = wfs
        if len(dram_ap.shape) == 3:
            wfd = wfs.rearrange("p (dc c) -> p dc c", c=dram_ap.shape[2])
        em.dma("sync", wfd, dram_ap, writes=[WFB[par]])
        em.emit("scalar", lambda h: h.copy(wh_ap, wfs),
                reads=[WFB[par]], writes=[wbuf] + list(extra))
        em.emit("vector", lambda h: h.tensor_tensor(wl_ap, wfs, wh_ap,
                                                    ALU.subtract),
                reads=[WFB[par], wbuf], writes=[wbufl, WFB[par]] + list(extra))

    def h_stage(j, whb, wlb, c1h, c1l, c2h, c2l, tis):
        toff = 128 * tis[0]
        for ts2 in range(2):
            rdh = [AB[t] for t in tis[4 * ts2:4 * ts2 + 4]]
            rdl = [ABL[t] for t in tis[4 * ts2:4 * ts2 + 4]]
            for wi, (cw, bb) in enumerate(((c1h, 4 + ts2), (c2h, 6 + ts2))):
                cl = c1l if cw is c1h else c2l
                for dc in range(8):
                    ah = actT_h[:, dc * TW + toff + 512 * ts2:
                                dc * TW + toff + 512 * (ts2 + 1)]
                    al = actT_l[:, dc * TW + toff + 512 * ts2:
                                dc * TW + toff + 512 * (ts2 + 1)]
                    for pi, (lw, rw, rb) in enumerate((
                            (cw, ah, [whb[wi]] + rdh),
                            (cw, al, [whb[wi]] + rdl),
                            (cl, ah, [wlb[wi]] + rdh))):
                        em.emit("tensor",
                                lambda h, dc=dc, bb=bb, lw=lw, rw=rw, pi=pi:
                                h.matmul(bank(bb),
                                         lw[:, 128 * dc:128 * (dc + 1)], rw,
                                         start=(dc == 0 and pi == 0),
                                         stop=(dc == 7 and pi == 2)),
                                reads=rb, writes=[PB[bb]])
        em.emit("scalar", lambda h: h.activation(
            out=h1s, in_=psum[:, 2048:3072], func=AF.Silu),
            reads=[PB[4], PB[5]], writes=[H1S, PB[4], PB[5]])
        em.emit("vector", lambda h: h.tensor_tensor(
            gst, psum[:, 3072:4096], h1s, ALU.mult),
            reads=[PB[6], PB[7], H1S], writes=[GSTB, PB[6], PB[7]])
        em.emit("scalar", lambda h, j=j: h.copy(
            hb_h[:, 1024 * j:1024 * (j + 1)], gst),
            reads=[GSTB], writes=[HB[j]])
        em.emit("vector", lambda h, j=j: h.tensor_tensor(
            hb_l[:, 1024 * j:1024 * (j + 1)], gst,
            hb_h[:, 1024 * j:1024 * (j + 1)], ALU.subtract),
            reads=[GSTB, HB[j]], writes=[HBL[j]])

    def out_stage(tis, gcol_fn):
        for k, ti in enumerate(tis):
            for dh in range(2):
                b = (2 * k + dh) % 4
                nmm = 0
                for j in range(2):
                    lh = hb_h[:, 1024 * j + 128 * k:1024 * j + 128 * (k + 1)]
                    ll = hb_l[:, 1024 * j + 128 * k:1024 * j + 128 * (k + 1)]
                    rh_ = w3h[:, 1024 * j + 512 * dh:1024 * j + 512 * (dh + 1)]
                    rl_ = w3l[:, 1024 * j + 512 * dh:1024 * j + 512 * (dh + 1)]
                    for lw, rw, rb in ((lh, rh_, [HB[j], W3C[j]]),
                                       (lh, rl_, [HB[j], W3L[j]]),
                                       (ll, rh_, [HBL[j], W3C[j]])):
                        nmm += 1
                        em.emit("tensor",
                                lambda h, lw=lw, rw=rw, b=b, s=(nmm == 1),
                                e=(nmm == 6):
                                h.matmul(bank(b), lw, rw, start=s, stop=e),
                                reads=rb, writes=[PB[b]])
                gcol, grd = gcol_fn(ti)
                xs = x_sb[:, 1024 * ti + 512 * dh:1024 * ti + 512 * (dh + 1)]
                em.emit("vector", lambda h, b=b, gcol=gcol, xs=xs:
                        h.scalar_tensor_tensor(xs, bank(b), gcol, xs,
                                               ALU.mult, ALU.add),
                        reads=[PB[b], XB[ti]] + grd, writes=[XB[ti], PB[b]])

    xev = xe_hbm.rearrange("(b p) c -> p b c", p=128)
    xgv = xg_hbm.rearrange("(b p) g -> p b g", p=128)
    xov = xo_hbm.rearrange("(b p) c -> p b c", p=128)
    xtv_h = actT_h[:, 0:6144].rearrange("p (dc t) -> p dc t", t=768)
    xtv_l = actT_l[:, 0:6144].rearrange("p (dc t) -> p dc t", t=768)
    hba_h = actT_h[:, 6144:9216]
    hba_l = actT_l[:, 6144:9216]
    w3s_h = actT_h[:, 9216:13312]
    w3s_l = actT_l[:, 9216:13312]
    AEH = Buf(); AEL = Buf()
    HBAH = [Buf() for _ in range(4)]
    HBAL = [Buf() for _ in range(4)]
    W3SH = [Buf() for _ in range(4)]
    W3SL = [Buf() for _ in range(4)]
    ACTW = AB + ABL + [AEH, AEL] + HBAH + HBAL + W3SH + W3SL
    ACTXTRA.extend([AEH, AEL] + HBAH + HBAL + W3SH + W3SL)

    def slot_stage(e):
        em.dma("sync", gsb.rearrange("p (b g) -> p b g", g=NE),
               xgv[:, 8 * e:8 * e + 6, :], reads=[XGB], writes=[GSBB])
        for st in range(6):
            hf = st % 2
            xst = srow[:, 1024 * hf:1024 * (hf + 1)]
            em.dma("sync", xst, xev[:, 8 * e + st, :],
                   reads=[XEB], writes=[SRH[0][hf]])
            for grp in range(2):
                b = 2 + grp
                for j in range(4):
                    dc = 4 * grp + j
                    em.emit("tensor", lambda h, dc=dc, j=j, b=b, xst=xst,
                            hf=hf:
                            h.transpose(bank(b)[:, 128 * j:128 * (j + 1)],
                                        xst[:, 128 * dc:128 * (dc + 1)],
                                        ident),
                            reads=[SRH[0][hf], CONST], writes=[PB[b]])
                em.emit("scalar", lambda h, grp=grp, st=st, b=b:
                        h.copy(xtv_h[:, 4 * grp:4 * grp + 4,
                                     128 * st:128 * (st + 1)],
                               bank(b).rearrange("p (dc t) -> p dc t", t=128)),
                        reads=[PB[b]], writes=[AEH] + ACTW)
                em.emit("vector", lambda h, grp=grp, st=st, b=b:
                        h.tensor_tensor(
                            xtv_l[:, 4 * grp:4 * grp + 4,
                                  128 * st:128 * (st + 1)],
                            bank(b).rearrange("p (dc t) -> p dc t", t=128),
                            xtv_h[:, 4 * grp:4 * grp + 4,
                                  128 * st:128 * (st + 1)],
                            ALU.subtract),
                        reads=[PB[b], AEH], writes=[AEL, PB[b]] + ACTW)

    def slot_h_stage(rr, c1h, c1l, c2h, c2l, whb, wlb):
        for si, (lo, wd) in enumerate(((0, 512), (512, 256))):
            for wi in range(2):
                cw = c1h if wi == 0 else c2h
                cl = c1l if wi == 0 else c2l
                bb = (4 if wi == 0 else 6) + si
                for dc in range(8):
                    ah = actT_h[:, dc * 768 + lo:dc * 768 + lo + wd]
                    al = actT_l[:, dc * 768 + lo:dc * 768 + lo + wd]
                    for pi, (lw, rw, rb) in enumerate((
                            (cw, ah, [whb[wi], AEH]),
                            (cw, al, [whb[wi], AEL]),
                            (cl, ah, [wlb[wi], AEH]))):
                        em.emit("tensor",
                                lambda h, dc=dc, bb=bb, lw=lw, rw=rw, pi=pi,
                                wd=wd:
                                h.matmul(bank(bb, wd),
                                         lw[:, 128 * dc:128 * (dc + 1)], rw,
                                         start=(dc == 0 and pi == 0),
                                         stop=(dc == 7 and pi == 2)),
                                reads=rb, writes=[PB[bb]])
        em.emit("scalar", lambda h: h.activation(
            out=h1s[:, 0:768], in_=psum[:, 2048:2816], func=AF.Silu),
            reads=[PB[4], PB[5]], writes=[H1S, PB[4], PB[5]])
        em.emit("vector", lambda h: h.tensor_tensor(
            gst[:, 0:768], psum[:, 3072:3840], h1s[:, 0:768], ALU.mult),
            reads=[PB[6], PB[7], H1S], writes=[GSTB, PB[6], PB[7]])
        em.emit("scalar", lambda h, rr=rr: h.copy(
            hba_h[:, 768 * rr:768 * (rr + 1)], gst[:, 0:768]),
            reads=[GSTB], writes=[HBAH[rr]] + ACTW)
        em.emit("vector", lambda h, rr=rr: h.tensor_tensor(
            hba_l[:, 768 * rr:768 * (rr + 1)], gst[:, 0:768],
            hba_h[:, 768 * rr:768 * (rr + 1)], ALU.subtract),
            reads=[GSTB, HBAH[rr]], writes=[HBAL[rr]] + ACTW)

    def slot_out_stage(e, rnd):
        for k in range(6):
            gcol = gsb[:, NE * k + e:NE * k + e + 1]
            hf = k % 2
            stage = srow_b[:, 1024 * hf:1024 * (hf + 1)]
            for dh in range(2):
                b = (2 * k + dh) % 2
                nmm = 0
                for rr in range(4):
                    lh = hba_h[:, 768 * rr + 128 * k:768 * rr + 128 * (k + 1)]
                    ll = hba_l[:, 768 * rr + 128 * k:768 * rr + 128 * (k + 1)]
                    rh_ = w3s_h[:, 1024 * rr + 512 * dh:
                                1024 * rr + 512 * (dh + 1)]
                    rl_ = w3s_l[:, 1024 * rr + 512 * dh:
                                1024 * rr + 512 * (dh + 1)]
                    for lw, rw, rb in ((lh, rh_, [HBAH[rr], W3SH[rr]]),
                                       (lh, rl_, [HBAH[rr], W3SL[rr]]),
                                       (ll, rh_, [HBAL[rr], W3SH[rr]])):
                        nmm += 1
                        em.emit("tensor",
                                lambda h, lw=lw, rw=rw, b=b, s=(nmm == 1),
                                ee=(nmm == 12):
                                h.matmul(bank(b), lw, rw, start=s, stop=ee),
                                reads=rb, writes=[PB[b]])
                em.emit("vector", lambda h, b=b, gcol=gcol, stage=stage,
                        dh=dh:
                        h.tensor_scalar(stage[:, 512 * dh:512 * (dh + 1)],
                                        bank(b), gcol, None, ALU.mult,
                                        ALU.bypass),
                        reads=[PB[b], GSBB], writes=[SRH[1][hf], PB[b]])
            if rnd == 0:
                em.dma("sync", xov[:, 8 * e + k, :], stage,
                       reads=[SRH[1][hf]], writes=[XOB])
            else:
                em.dma_op("gpsimd", lambda h, e=e, k=k, stage=stage:
                          h.dma_start(out=xov[:, 8 * e + k, :], in_=stage,
                                      accum_op=ALU.add),
                          reads=[SRH[1][hf], XOB], writes=[XOB])

    def routed_sparse():
        w12vs = [dw12T[e].rearrange("(dc p) hh -> p dc hh", p=128)
                 for e in range(NE)]
        for e in range(NE):
            slot_stage(e)
            for rnd in range(2):
                for rr in range(4):
                    rhc = 4 * rnd + rr
                    j = rr % 2
                    c1h = wsh[:, 2048 * j:2048 * j + 1024]
                    c1l = wsl[:, 2048 * j:2048 * j + 1024]
                    c2h = wsh[:, 2048 * j + 1024:2048 * (j + 1)]
                    c2l = wsl[:, 2048 * j + 1024:2048 * (j + 1)]
                    wload_split(w12vs[e][:, :, 128 * rhc:128 * (rhc + 1)],
                                c1h, c1l, WS4[2 * j], WS4L[2 * j])
                    wload_split(
                        w12vs[e][:, :, RH + 128 * rhc:RH + 128 * (rhc + 1)],
                        c2h, c2l, WS4[2 * j + 1],
                        WS4L[2 * j + 1])
                    slot_h_stage(rr, c1h, c1l, c2h, c2l,
                                 (WS4[2 * j], WS4[2 * j + 1]),
                                 (WS4L[2 * j], WS4L[2 * j + 1]))
                    wload_split(dw3eT[e][128 * rhc:128 * (rhc + 1), :],
                                w3s_h[:, 1024 * rr:1024 * (rr + 1)],
                                w3s_l[:, 1024 * rr:1024 * (rr + 1)],
                                W3SH[rr], W3SL[rr], extra=ACTW)
                slot_out_stage(e, rnd)

    def ffn_call():
        for ti in range(NT):
            ln_tile(ti)
            router_tile(ti)
            route_tile(ti)
        w1v = dw1T.rearrange("(dc p) hh -> p dc hh", p=128)
        w2v = dw2T.rearrange("(dc p) hh -> p dc hh", p=128)
        for half in range(2):
            tis = list(range(8 * half, 8 * half + 8))
            for g in range(16):
                for j in range(2):
                    hc = 2 * g + j
                    c1h = wsh[:, 1024 * j:1024 * (j + 1)]
                    c1l = wsl[:, 1024 * j:1024 * (j + 1)]
                    c2h = wsh[:, 2048 + 1024 * j:3072 + 1024 * j]
                    c2l = wsl[:, 2048 + 1024 * j:3072 + 1024 * j]
                    wload_split(
                        w1v[:, :, 128 * hc:128 * (hc + 1)], c1h, c1l,
                        WS4[j], WS4L[j])
                    wload_split(
                        w2v[:, :, 128 * hc:128 * (hc + 1)], c2h, c2l,
                        WS4[2 + j], WS4L[2 + j])
                    h_stage(j, (WS4[j], WS4[2 + j]), (WS4L[j], WS4L[2 + j]),
                            c1h, c1l, c2h, c2l, tis)
                    wload_split(dw3sT[128 * hc:128 * (hc + 1), :],
                                w3h[:, 1024 * j:1024 * (j + 1)],
                                w3l[:, 1024 * j:1024 * (j + 1)],
                                W3C[j], W3L[j])
                out_stage(tis, lambda ti: (0.5, []))
            pass
        routed_sparse()
        for ti in range(NT):
            for rank in range(2):
                oc = offs[:, 2 * ti + rank:2 * ti + rank + 1]
                xs = x_sb[:, 1024 * ti:1024 * (ti + 1)]
                em.dma_op("gpsimd", lambda h, oc=oc, xs=xs:
                          h.indirect_dma_start(xs, None, xo_hbm,
                                               IndirectOffsetOnAxis(oc, 0),
                                               compute_op=ALU.add),
                          reads=[XOB, OFFB, XB[ti]], writes=[XB[ti]])

    def attn_block():
        for ti in range(NT):
            ln_tile(ti)
        slopes = [2.0 ** (-(hh + 1) / 2.0) for hh in range(HEADS)]
        for p in range(8):
            wslices = []
            for qi_, dw in enumerate((dwqT, dwkT, dwvT)):
                wvw = dw.rearrange("(dc p2) o -> p2 dc o", p2=128)
                ch = wsh[:, 1024 * qi_:1024 * (qi_ + 1)]
                cl = wsl[:, 1024 * qi_:1024 * (qi_ + 1)]
                wload_split(
                    wvw[:, :, 128 * p:128 * (p + 1)], ch, cl,
                    WS4[qi_], WS4L[qi_])
                wslices.append((ch, cl))
            woh = wsh[:, 3072:4096]
            wol = wsl[:, 3072:4096]
            wload_split(dwoT[128 * p:128 * (p + 1), :], woh, wol,
                        WS4[3], WS4L[3])
            (wqh, wql), (wkh, wkl), (wvh, wvl) = wslices
            for ts4 in range(4):
                rdh = [AB[t] for t in range(4 * ts4, 4 * ts4 + 4)]
                rdl = [ABL[t] for t in range(4 * ts4, 4 * ts4 + 4)]
                for wi, (dsth, dstl, hbl, lbl, boff) in enumerate(
                        ((qth, qtl, [HB[0], HB[1]], [HBL[0], HBL[1]], 4),
                         (kth, ktl, [W3C[0], W3C[1]], [W3L[0], W3L[1]], 6))):
                    wh_ = wsh[:, 1024 * wi:1024 * (wi + 1)]
                    wl_ = wsl[:, 1024 * wi:1024 * (wi + 1)]
                    b = boff + ts4 % 2
                    for dc in range(8):
                        ah = actT_h[:, dc * TW + 512 * ts4:
                                    dc * TW + 512 * (ts4 + 1)]
                        al = actT_l[:, dc * TW + 512 * ts4:
                                    dc * TW + 512 * (ts4 + 1)]
                        for pi, (lw, rw, rb) in enumerate(
                                ((wh_, ah, [WS4[wi]] + rdh),
                                 (wh_, al, [WS4[wi]] + rdl),
                                 (wl_, ah, [WS4L[wi]] + rdh))):
                            em.emit("tensor",
                                    lambda h, dc=dc, b=b, lw=lw, rw=rw, pi=pi:
                                    h.matmul(bank(b),
                                             lw[:, 128 * dc:128 * (dc + 1)],
                                             rw,
                                             start=(dc == 0 and pi == 0),
                                             stop=(dc == 7 and pi == 2)),
                                    reads=rb, writes=[PB[b]])
                    em.emit("scalar", lambda h, dsth=dsth, ts4=ts4, b=b:
                            h.copy(dsth[:, 512 * ts4:512 * (ts4 + 1)],
                                   bank(b)),
                            reads=[PB[b]], writes=list(hbl))
                    em.emit("vector",
                            lambda h, dsth=dsth, dstl=dstl, ts4=ts4, b=b:
                            h.tensor_tensor(dstl[:, 512 * ts4:512 * (ts4 + 1)],
                                            bank(b),
                                            dsth[:, 512 * ts4:512 * (ts4 + 1)],
                                            ALU.subtract),
                            reads=[PB[b]] + list(hbl),
                            writes=list(lbl) + [PB[b]])
            for k in range(NT):
                b = k % 2
                for dc in range(8):
                    ah = athv[:, dc, 128 * k:128 * (k + 1)]
                    al = atlv[:, dc, 128 * k:128 * (k + 1)]
                    vh = wvh[:, 128 * dc:128 * (dc + 1)]
                    vl = wvl[:, 128 * dc:128 * (dc + 1)]
                    for pi, (lw, rw, rb) in enumerate(
                            ((ah, vh, [WS4[2], AB[k]]),
                             (ah, vl, [WS4L[2], AB[k]]),
                             (al, vh, [WS4[2], ABL[k]]))):
                        em.emit("tensor",
                                lambda h, b=b, lw=lw, rw=rw, dc=dc, pi=pi:
                                h.matmul(bank(b, 128), lw, rw,
                                         start=(dc == 0 and pi == 0),
                                         stop=(dc == 7 and pi == 2)),
                                reads=rb, writes=[PB[b]])
                em.emit("scalar", lambda h, k=k, b=b: h.copy(
                    v_sb[:, 128 * k:128 * (k + 1)], bank(b, 128)),
                    reads=[PB[b]], writes=[VBL[k // 8], PB[b]])
            def emit_scores(qi, hi, par):
                hof = 64 * hi
                slope = slopes[2 * p + hi]
                sr = srow if par == 0 else srow_b
                kmax = 128 * (qi + 1)
                nkj = ceil_div(kmax, 512)
                em.emit("vector", lambda h, slope=slope: h.tensor_scalar(
                    rs512, iotaD, slope, None, ALU.mult, ALU.bypass),
                    reads=[CONST], writes=[RSB])
                for kj in range(nkj):
                    w = min(512, kmax - 512 * kj)
                    b = 4 + kj
                    qh_ = qth[hof:hof + 64, 128 * qi:128 * (qi + 1)]
                    ql_ = qtl[hof:hof + 64, 128 * qi:128 * (qi + 1)]
                    kh_ = kth[hof:hof + 64, 512 * kj:512 * kj + w]
                    kl_ = ktl[hof:hof + 64, 512 * kj:512 * kj + w]
                    for pi, (lw, rw) in enumerate(
                            ((qh_, kh_), (qh_, kl_), (ql_, kh_))):
                        em.emit("tensor",
                                lambda h, w=w, b=b, lw=lw, rw=rw, pi=pi:
                                h.matmul(bank(b, w), lw, rw,
                                         start=(pi == 0), stop=(pi == 2)),
                                reads=QTBL + KTBL, writes=[PB[b]])
                    em.emit("vector", lambda h, kj=kj, w=w, b=b, s=slope,
                            sr=sr:
                            h.tensor_scalar(
                                sr[:, 512 * kj:512 * kj + w],
                                bank(b, w), 0.125, s * 512.0 * kj,
                                ALU.mult, ALU.add),
                            reads=[PB[b]], writes=SRH[par] + [PB[b]])
                    em.emit("vector", lambda h, kj=kj, w=w, sr=sr:
                            h.tensor_tensor(
                                sr[:, 512 * kj:512 * kj + w],
                                sr[:, 512 * kj:512 * kj + w],
                                rs512[:, 0:w], ALU.add),
                            reads=SRH[par] + [RSB], writes=SRH[par])
                    if kj == nkj - 1:
                        em.emit("vector", lambda h, kmax=kmax, sr=sr:
                                h.tensor_tensor(sr[:, kmax - 128:kmax],
                                                sr[:, kmax - 128:kmax],
                                                mask512, ALU.add),
                                reads=SRH[par] + [CONST], writes=SRH[par])
                    em.emit("vector", lambda h, kj=kj, w=w, sr=sr, par=par:
                            h.reduce_max(mch[:, 4 * par + kj:4 * par + kj + 1],
                                         sr[:, 512 * kj:512 * kj + w],
                                         axis=AX.X),
                            reads=SRH[par], writes=[STQ[par]])
                em.emit("vector", lambda h, par=par, nkj=nkj: h.reduce_max(
                    mrow, mch[:, 4 * par:4 * par + nkj], axis=AX.X),
                    reads=[STQ[par]], writes=[STQ[par]])
                em.emit("vector", lambda h, par=par: h.tensor_scalar(
                    negm2[:, par:par + 1], mrow, -1.0, None, ALU.mult,
                    ALU.bypass),
                    reads=[STQ[par]], writes=[STQ[par]])

            def emit_epilogue(qi, hi, par):
                hof = 64 * hi
                sr = srow if par == 0 else srow_b
                kmax = 128 * (qi + 1)
                nkj = ceil_div(kmax, 512)
                nkt = kmax // 128
                for kj in range(nkj):
                    w = min(512, kmax - 512 * kj)
                    em.emit("scalar", lambda h, kj=kj, w=w, sr=sr, par=par:
                            h.activation(out=sr[:, 512 * kj:512 * kj + w],
                                         in_=sr[:, 512 * kj:512 * kj + w],
                                         func=AF.Exp,
                                         bias=negm2[:, par:par + 1], scale=1.0,
                                         accum_out=sch[:, kj:kj + 1]),
                            reads=SRH[par] + [STQ[par]],
                            writes=SRH[par] + [SCHB])
                em.emit("vector", lambda h, nkj=nkj: h.tensor_reduce(
                    sumr, sch[:, 0:nkj], axis=AX.X, op=ALU.add),
                    reads=[SCHB], writes=[SCHB])
                em.emit("vector", lambda h, hi=hi: h.reciprocal(
                    recip[:, hi:hi + 1], sumr),
                    reads=[SCHB], writes=[RECB])
                for kt4 in range(ceil_div(nkt, 4)):
                    b = kt4 % 2
                    n = min(4, nkt - 4 * kt4)
                    sl = PT2[kt4 % 2]
                    for j in range(n):
                        kt = 4 * kt4 + j
                        em.emit("tensor", lambda h, kt=kt, j=j, b=b, sr=sr:
                                h.transpose(
                                    bank(b)[:, 128 * j:128 * (j + 1)],
                                    sr[:, 128 * kt:128 * (kt + 1)],
                                    ident),
                                reads=SRH[par] + [CONST], writes=[PB[b]])
                    em.emit("vector", lambda h, kt4=kt4, b=b, n=n:
                            h.tensor_copy(
                                ptb[:, 512 * (kt4 % 2):
                                    512 * (kt4 % 2) + 128 * n],
                                bank(b, 128 * n)),
                            reads=[PB[b]], writes=[sl, PB[b]])
                    for j in range(n):
                        kt = 4 * kt4 + j
                        em.emit("tensor",
                                lambda h, kt=kt, j=j, kt4=kt4, hof=hof,
                                hi=hi, nkt=nkt:
                                h.matmul(
                                    bank(2, 128)[:, 64 * hi:64 * (hi + 1)],
                                    ptb[:, 512 * (kt4 % 2) + 128 * j:
                                        512 * (kt4 % 2) + 128 * (j + 1)],
                                    v_sb[:, 128 * kt + hof:
                                         128 * kt + hof + 64],
                                    start=(kt == 0), stop=(kt == nkt - 1)),
                                reads=[sl, VBL[kt // 8]], writes=[PB[2]])
                em.emit("vector", lambda h, hi=hi: h.tensor_scalar(
                    ocomb[:, 64 * hi:64 * (hi + 1)],
                    bank(2, 128)[:, 64 * hi:64 * (hi + 1)],
                    recip[:, hi:hi + 1], None, ALU.mult, ALU.bypass),
                    reads=[PB[2], RECB], writes=[OCB, PB[2]])
                if hi == 1:
                    em.emit("tensor", lambda h: h.transpose(bank(3, 128),
                                                            ocomb, ident),
                            reads=[OCB, CONST], writes=[PB[3]])
                    em.emit("scalar", lambda h: h.copy(otc_h, bank(3, 128)),
                            reads=[PB[3]], writes=[OTB])
                    em.emit("vector", lambda h: h.tensor_tensor(
                        otc_l, bank(3, 128), otc_h, ALU.subtract),
                        reads=[PB[3], OTB], writes=[OTBL, PB[3]])
                    for dh in range(2):
                        rh_ = woh[:, 512 * dh:512 * (dh + 1)]
                        rl_ = wol[:, 512 * dh:512 * (dh + 1)]
                        for pi, (lw, rw, rb) in enumerate(
                                ((otc_h, rh_, [OTB, WS4[3]]),
                                 (otc_h, rl_, [OTB, WS4L[3]]),
                                 (otc_l, rh_, [OTBL, WS4[3]]))):
                            em.emit("tensor",
                                    lambda h, lw=lw, rw=rw, pi=pi:
                                    h.matmul(bank(3), lw, rw,
                                             start=(pi == 0), stop=(pi == 2)),
                                    reads=rb, writes=[PB[3]])
                        xs = x_sb[:, 1024 * qi + 512 * dh:
                                  1024 * qi + 512 * (dh + 1)]
                        em.emit("vector", lambda h, xs=xs:
                                h.scalar_tensor_tensor(xs, bank(3), 1.0, xs,
                                                       ALU.mult, ALU.add),
                                reads=[PB[3], XB[qi]], writes=[XB[qi], PB[3]])

            its = [(qi, hi) for qi in range(NT) for hi in range(2)]
            prev = None
            for idx, (qi, hi) in enumerate(its):
                emit_scores(qi, hi, idx % 2)
                if prev is not None:
                    emit_epilogue(prev[0], prev[1], (idx - 1) % 2)
                prev = (qi, hi)
            emit_epilogue(prev[0], prev[1], (len(its) - 1) % 2)
    ffn_call()
    attn_block()
    ffn_call()
    ov = dout.rearrange("(t p) d -> p t d", p=128)
    for ti in range(NT):
        em.dma("gpsimd", ov[:, ti, :], x_sb[:, 1024 * ti:1024 * (ti + 1)],
               reads=[XB[ti]])
    final_q = em.dma_cnt["qG"]

    stack = contextlib.ExitStack()
    sems = {}
    for name in ["s_tensor", "s_vector", "s_scalar", "s_gpsimd", "s_sync",
                 "qS", "qG", "qA"]:
        sems[name] = stack.enter_context(nc.semaphore(name))
    block = stack.enter_context(nc.Block())
    fin = Op("gpsimd", lambda h: h.nop(), len(em.ops["gpsimd"]))
    fin.waits_dma["qG"] = final_q
    em.ops["gpsimd"].append(fin)
    em.finalize_and_replay(block, sems)
    stack.close()

    nc.m.detect_race_conditions = False
    return nc


_CACHE = {}


def _get_nc():
    if "full" not in _CACHE:
        _CACHE["full"] = build_program()
    return _CACHE["full"]


def kernel(L0_attn_x, L0_attn_W_Q, L0_attn_W_K, L0_attn_W_V, L0_attn_W_O,
           L0_ffn_w12, L0_ffn_w3, L0_ffn_shared_w1, L0_ffn_shared_w2,
           L0_ffn_shared_w3, L0_ffn_router_w, L0_ffn_expert_bias):
    x = np.ascontiguousarray(np.asarray(L0_attn_x, dtype=np.float32))

    def T(a):
        return np.ascontiguousarray(np.asarray(a, dtype=np.float32).T)

    w = {
        "wqT": T(L0_attn_W_Q), "wkT": T(L0_attn_W_K), "wvT": T(L0_attn_W_V),
        "woT": T(L0_attn_W_O),
        "w1T": T(L0_ffn_shared_w1), "w2T": T(L0_ffn_shared_w2),
        "w3sT": T(L0_ffn_shared_w3),
        "w12T": np.ascontiguousarray(
            np.asarray(L0_ffn_w12, dtype=np.float32).transpose(0, 2, 1)),
        "w3eT": np.ascontiguousarray(
            np.asarray(L0_ffn_w3, dtype=np.float32).transpose(0, 2, 1)),
        "routerT": T(L0_ffn_router_w),
        "ebias": np.ascontiguousarray(
            np.asarray(L0_ffn_expert_bias, dtype=np.float32).reshape(1, NE)),
    }
    nc = _get_nc()
    in_maps = [dict(w, x_in=np.ascontiguousarray(x[b])) for b in range(NCORES)]
    res = run_bass_kernel_spmd(nc, in_maps, core_ids=list(range(NCORES)))
    out = np.stack([res.results[b]["x_out"] for b in range(NCORES)], axis=0)
    return out.astype(np.float32)
